# version 35
# baseline (speedup 1.0000x reference)
import numpy as np
from contextlib import ExitStack

import concourse.bass as bass
import concourse.mybir as mybir
from concourse.bass_utils import run_bass_kernel_spmd

F32 = mybir.dt.float32
BF16 = mybir.dt.bfloat16
AF = mybir.ActivationFunctionType
ALU = mybir.AluOpType
AX = mybir.AxisListType

N_CORES = 8
EPS_LN = 1e-5
EPS_ATT = 1e-6


class Emitter:
    ENGS = ("sync", "gpsimd", "vector", "scalar", "tensor")

    def __init__(self):
        self.ops = {e: [] for e in self.ENGS}
        self.cnt = {}

    def op(self, eng, fn):
        self.ops[eng].append(fn)

    def bump(self, sem, k):
        self.cnt[sem] = self.cnt.get(sem, 0) + k
        return self.cnt[sem]

    def cur(self, sem):
        return self.cnt.get(sem, 0)


class Res:

    def __init__(self):
        self.prod = None
        self.users = []


def build_program(D=2048, TOWN=1024, DTILE=512, debug=False):
    KC = D // 128
    H = D // 128
    TC = TOWN // 128
    SEQ = 2 * TOWN
    NT = min(512, TOWN)
    NTT = TOWN // NT
    NDT = D // DTILE
    HPD = DTILE // 128

    nc = bass.Bass("TRN2", target_bir_lowering=False)

    ap = {}

    def dram(name, shape, dt=F32, kind="ExternalInput"):
        ap[name] = nc.dram_tensor(name, list(shape), dt, kind=kind).ap()

    dram("xT", (D, TOWN))
    dram("xoT", (D, TOWN))
    for w in ("wqT", "wkT", "wvT", "woT", "wsT"):
        dram(w, (D, D))
    for g in ("bq_g", "bo_g", "lnw_g", "lnb_g", "gam_g", "bet_g"):
        dram(g, (128, KC))
    dram("bk_r", (1, D))
    dram("bv_r", (1, D))
    dram("posf_g", (128, 2 * TC))
    dram("ident", (128, 128))
    dram("sr0", (2 * TOWN // 128, 128), F32, kind="Internal")
    dram("sr1", (2 * TOWN // 128, 128), F32, kind="Internal")
    dram("yT", (D, TOWN), kind="ExternalOutput")
    if debug:
        dram("d_sacc", (128, 4 * TC), kind="ExternalOutput")
        dram("d_gdm", (128, 2 * TC), kind="ExternalOutput")
        dram("d_gsv", (128, 2 * TC), kind="ExternalOutput")
        dram("d_l2", (2, SEQ), kind="ExternalOutput")
        dram("d_xn", (128, KC * TOWN), kind="ExternalOutput")
        dram("d_kv", (128, (D // 128) * 129), kind="ExternalOutput")
        dram("d_xn2", (128, KC * TOWN), kind="ExternalOutput")
        dram("d_out", (128, KC * TOWN), kind="ExternalOutput")
        dram("d_graw", (128, KC * TOWN), kind="ExternalOutput")
        dram("d_vex", (128, (DTILE // 128) * 129), kind="ExternalOutput")
        dram("d_kt", (128, DTILE), kind="ExternalOutput")

    P = Emitter()
    SEMS = ("ld", "lt0", "lt1", "lx0", "lx1", "lw0", "lw1", "lr0", "lr1",
            "wx", "wl0", "wl1", "wl2", "wl3", "wl4", "gx0", "gx1", "qw0",
            "qw1", "qw2", "ow0", "ow1", "ow2", "sa", "sb",
            "pe", "dv", "ac", "gp", "st0", "st1")
    DMA_SEMS = ("ld", "lt0", "lt1", "lx0", "lx1", "lw0", "lw1", "lr0", "lr1",
                "wx", "wl0", "wl1", "wl2", "wl3", "wl4", "gx0", "gx1", "qw0",
                "qw1", "qw2", "ow0", "ow1", "ow2", "sa", "sb",
                "st0", "st1")

    with ExitStack() as es0:
        e0 = es0.enter_context
        sem = {s: e0(nc.semaphore(s)) for s in SEMS}

        rowsMV = e0(nc.sbuf_tensor("rowsMV", [1, 2 * TOWN], F32))
        grids = {}
        for g in ("bq_g", "bo_g", "lnw_g", "lnb_g", "gam_g", "bet_g", "posf_g"):
            w = 2 * TC if g == "posf_g" else KC
            grids[g] = e0(nc.sbuf_tensor(g + "_s", [128, w], F32))
        decf = e0(nc.sbuf_tensor("decf", [128, 2 * TC], F32))
        idents = e0(nc.sbuf_tensor("idents", [128, 128], F32))
        identb = e0(nc.sbuf_tensor("identb", [128, 128], BF16))
        ones1f = e0(nc.sbuf_tensor("ones1f", [1, 128], F32))
        KVbf = e0(nc.sbuf_tensor("KVbf", [128, H * 129], BF16))
        sden = e0(nc.sbuf_tensor("sden", [128, H], F32))
        stok = e0(nc.sbuf_tensor("stok", [128, H], F32))
        gsc_g = e0(nc.sbuf_tensor("gsc_g", [128, KC], F32))
        s11c = e0(nc.sbuf_tensor("s11c", [128, 1], F32))
        maxg = e0(nc.sbuf_tensor("maxg", [128, KC + 1], F32))
        s11r = e0(nc.sbuf_tensor("s11r", [1, 130], F32))

        pb = [e0(nc.psum_tensor(f"pb{i}", [128, 512], F32)) for i in range(8)]
        bank = [Res() for _ in range(8)]

        def W(eng, s, n):
            if n > 0:
                P.op(eng, lambda E, s=s, n=n: E.wait_ge(sem[s], n))

        def waits(eng, deps):
            need = {}
            for s, n in deps:
                if s in DMA_SEMS:
                    n = P.cur(s)
                if n > need.get(s, -1):
                    need[s] = n
            for s, n in need.items():
                W(eng, s, n)

        def ve(fn, deps=()):
            waits("vector", deps)
            W("vector", "dv", P.cur("dv"))
            P.op("vector", lambda E, fn=fn: fn(E).then_inc(sem["dv"], 1))
            return ("dv", P.bump("dv", 1))

        def gp(fn, deps=()):
            waits("gpsimd", deps)
            W("gpsimd", "gp", P.cur("gp"))
            P.op("gpsimd", lambda E, fn=fn: fn(E).then_inc(sem["gp"], 1))
            return ("gp", P.bump("gp", 1))

        def sc(fn, deps=()):
            waits("scalar", deps)
            W("scalar", "ac", P.cur("ac"))
            P.op("scalar", lambda E, fn=fn: fn(E).then_inc(sem["ac"], 1))
            return ("ac", P.bump("ac", 1))

        def te(fn, deps=(), inc=True):
            waits("tensor", deps)
            if inc:
                P.op("tensor", lambda E, fn=fn: fn(E).then_inc(sem["pe"], 1))
                return ("pe", P.bump("pe", 1))
            P.op("tensor", lambda E, fn=fn: fn(E))
            return ("pe", P.cur("pe") + 1)

        def dma(eng, out, in_, s_name, deps=()):
            waits(eng, deps)
            P.op(eng, lambda E, out=out, in_=in_: E.dma_start(out=out, in_=in_).then_inc(sem[s_name], 16))
            return (s_name, P.bump(s_name, 16))

        def wslice_dma(wname, dst, c0, width, s_name, deps=(), eng="gpsimd", nsplit=4):
            waits(eng, deps)
            last = None
            step = KC // nsplit
            for j in range(nsplit):
                src = ap[wname].rearrange("(kc p) c -> p kc c", p=128)[
                    :, j * step:(j + 1) * step, c0:c0 + width]
                d = dst[:, 0:KC * width].rearrange("p (kc c) -> p kc c", c=width)[
                    :, j * step:(j + 1) * step, :]
                P.op(eng, lambda E, d=d, src=src: E.dma_start(out=d, in_=src).then_inc(sem[s_name], 16))
                last = (s_name, P.bump(s_name, 16))
            return last

        def bank_take(i, deps=()):
            waits("tensor", bank[i].users + ([bank[i].prod] if bank[i].prod else []) + list(deps))
            bank[i].users = []

        def bank_prod(i, tag):
            bank[i].prod = tag

        def bank_dep(i):
            return [bank[i].prod] if bank[i].prod else []

        def bank_use(i, tag):
            bank[i].users.append(tag)

        def barrier():
            tgt = {s: P.cur(s) for s in SEMS}
            for eng in P.ENGS:
                for s, n in tgt.items():
                    W(eng, s, n)
            for b in bank:
                b.prod = None
                b.users = []

        def bcast_rows(pairs, banks):
            jj = 0
            for (srow, dst_b, dep, dst_r) in pairs:
                w = srow.shape[-1]
                first = True
                for c0 in range(0, w, 512):
                    cw = min(512, w - c0)
                    bi = banks[jj % 2]
                    jj += 1
                    bank_take(bi, deps=([dep] if dep else []) + [("dv", P.cur("dv"))])
                    dM = te(lambda E, srow=srow, bi=bi, c0=c0, cw=cw: E.matmul(
                        out=pb[bi][:, 0:cw], lhsT=ones1f[:, :], rhs=srow[:, c0:c0 + cw],
                        start=True, stop=True))
                    bank_prod(bi, dM)
                    dC = ve(lambda E, dst_b=dst_b, bi=bi, c0=c0, cw=cw: E.tensor_copy(
                        out=dst_b[:, c0:c0 + cw], in_=pb[bi][:, 0:cw]),
                        deps=bank_dep(bi) + (dst_r.users if first else []))
                    first = False
                    bank_use(bi, dC)
                dst_r.users = []
                dst_r.prod = dC

        for g in ("bq_g", "bo_g", "lnw_g", "lnb_g", "gam_g", "bet_g", "posf_g"):
            dma("sync", grids[g][:, :], ap[g][:, :], "ld")
        dma("sync", idents[:, :], ap["ident"][:, :], "ld")
        ld_consts = ("ld", P.cur("ld"))

        ve(lambda E: E.memset(ones1f[:, :], 1.0))
        d_idb = ve(lambda E: E.tensor_copy(out=identb[:, :], in_=idents[:, :]), deps=[ld_consts])
        d_dec = sc(lambda E: E.activation(out=decf[:, :], in_=grids["posf_g"][:, :],
                                          func=AF.Exp, scale=-1.0 / SEQ), deps=[ld_consts])

        HBW = D // 2
        KC2 = KC // 2
        es_pre = ExitStack()
        xn_own = es_pre.enter_context(nc.sbuf_tensor("xn_own", [128, KC * TOWN], BF16))
        es_pre2 = ExitStack()
        epre = es_pre2.enter_context
        tacc = epre(nc.sbuf_tensor("tacc", [2 * TC, 128], F32))
        tacc_r = Res()
        sacc = epre(nc.sbuf_tensor("sacc", [128, 4 * TC], F32))
        gdm = epre(nc.sbuf_tensor("gdm", [128, 2 * TC], F32))
        gsv = epre(nc.sbuf_tensor("gsv", [128, 2 * TC], F32))
        gr1 = epre(nc.sbuf_tensor("gr1", [128, 2 * TC], F32))
        grd = epre(nc.sbuf_tensor("grd", [128, 2 * TC], F32))
        rowsrc = epre(nc.sbuf_tensor("rowsrc", [128, 2 * TC], F32))
        lhsT2 = epre(nc.sbuf_tensor("lhsT2", [2, SEQ], BF16))
        onescol = epre(nc.sbuf_tensor("onescol", [128, 1], BF16))
        Mb = epre(nc.sbuf_tensor("Mb", [128, NT], F32))
        mb_r = Res()
        R1b = epre(nc.sbuf_tensor("R1b", [128, NT], F32))
        r1_r = Res()

        es3 = ExitStack()
        e3 = es3.enter_context
        WkB = [e3(nc.sbuf_tensor(f"WkB{i}", [128, KC * DTILE], BF16)) for i in range(2)]
        WvB = [e3(nc.sbuf_tensor(f"WvB{i}", [128, KC * DTILE], BF16)) for i in range(2)]
        wkr = [Res(), Res()]
        wvr = [Res(), Res()]
        Ktile = [e3(nc.sbuf_tensor(f"Ktile{i}", [128, DTILE], BF16)) for i in range(2)]
        kt_r = [Res(), Res()]
        Vext = [e3(nc.sbuf_tensor(f"Vext{i}", [128, HPD * 129], BF16)) for i in range(2)]
        vx_r = [Res(), Res()]
        krhs2 = e3(nc.sbuf_tensor("krhs2", [2, D], BF16))
        vrhs2 = e3(nc.sbuf_tensor("vrhs2", [2, D], BF16))
        d_bk = dma("gpsimd", krhs2[1:2, :], ap["bk_r"][:, :], "wx")
        d_bv = dma("gpsimd", vrhs2[1:2, :], ap["bv_r"][:, :], "wx")
        for i in range(2):
            ve(lambda E, i=i: E.memset(
                Vext[i][:, :].rearrange("p (h c) -> p h c", c=129)[:, :, 128:129], 1.0))
        d_ones = ve(lambda E: E.memset(onescol[:, :], 1.0))

        es_xb = ExitStack()
        xbf = es_xb.enter_context(nc.sbuf_tensor("xbf", [128, KC * SEQ], BF16))
        es_sl = ExitStack()
        xf = [es_sl.enter_context(nc.sbuf_tensor(f"xf{i}", [128, D // 2], F32))
              for i in range(2)]
        xf_r = [Res(), Res()]
        xsq = [es_sl.enter_context(nc.sbuf_tensor(f"xsq{i}", [128, D // 2], BF16))
               for i in range(2)]

        ksem = ("wl0", "wl1")
        vsem = ("wl2", "wl3")
        dW = {}

        def load_kv_slices(dt, extra=()):
            p = dt % 2
            dWk = wslice_dma("wkT", WkB[p], dt * DTILE, DTILE, ksem[p],
                             deps=wkr[p].users + list(extra))
            wkr[p].users = []
            dWv = wslice_dma("wvT", WvB[p], dt * DTILE, DTILE, vsem[p], deps=wvr[p].users)
            wvr[p].users = []
            dW[dt] = (dWk, dWv)

        load_kv_slices(0)

        wc_r = [Res(), Res(), Res()]
        qsem = ("qw0", "qw1", "qw2")
        dWq = {}
        Wcol = []
        es_q2 = ExitStack()

        def load_q(dq):
            b = dq % 3
            dWq[dq] = wslice_dma("wqT", Wcol[b], dq * 128, 128, qsem[b], deps=wc_r[b].users)
            wc_r[b].users = []

        NQ = 4
        KCq = KC // NQ
        d_sl = {}
        last_evac = {}
        sq_r = [Res(), Res()]

        def emit_slab2(ps, banks4):
            bAm, bAs, bBm, bBs = banks4
            sgA = 2 * ps
            own = sgA < TC
            src = "xT" if own else "xoT"
            pl = (sgA % TC) // 2
            scolA = sgA * 128
            off = 0 if own else 2 * TC
            casts = []
            for b in banks4:
                bank_take(b)
            dMs = {0: None, 1: None}
            dSs = {0: None, 1: None}
            for q in range(NQ):
                p = q % 2
                eng_d, semn = ("sync", "lt0") if p == 0 else ("scalar", "lt1")
                dL = dma(eng_d, xf[p][:, :].rearrange("p (kc t) -> p kc t", t=256),
                         ap[src].rearrange("(kc p) t -> p kc t", p=128)[
                             :, q * KCq:(q + 1) * KCq, pl * 256:(pl + 1) * 256],
                         semn, deps=xf_r[p].users)
                xf_r[p].users = []
                dC = sc(lambda E, p=p, q=q, scolA=scolA: E.copy(
                    out=xbf[:, :].rearrange("p (kc t) -> p kc t", t=SEQ)[
                        :, q * KCq:(q + 1) * KCq, scolA:scolA + 256],
                    in_=xf[p][:, :].rearrange("p (kc t) -> p kc t", t=256)),
                    deps=[dL])
                dQ = sc(lambda E, p=p: E.activation(
                    out=xsq[p][:, :], in_=xf[p][:, :], func=AF.Square),
                    deps=sq_r[p].users)
                sq_r[p].users = []
                xf_r[p].users.append(dQ)
                casts.append(dC)
                for k2 in range(KCq):
                    kc = q * KCq + k2
                    for sub in range(2):
                        sg = sgA + sub
                        bm = bAm if sub == 0 else bBm
                        bs = bAs if sub == 0 else bBs
                        col = k2 * 256 + sub * 128
                        dMs[sub] = te(lambda E, kc=kc, sg=sg, bm=bm: E.matmul(
                            out=pb[bm][:, 0:1],
                            lhsT=xbf[:, kc * SEQ + sg * 128:kc * SEQ + sg * 128 + 128],
                            rhs=onescol[:, :], start=(kc == 0), stop=(kc == KC - 1)),
                            inc=(kc == KC - 1),
                            deps=([dC, d_ones] if k2 == 0 else []))
                        dSs[sub] = te(lambda E, kc=kc, col=col, p=p, bs=bs: E.matmul(
                            out=pb[bs][:, 1:2],
                            lhsT=xsq[p][:, col:col + 128],
                            rhs=onescol[:, :], start=(kc == 0), stop=(kc == KC - 1)),
                            inc=True,
                            deps=([dQ] if k2 == 0 else []))
                sq_r[p].users.append(dSs[1])
            for sub in range(2):
                sg = sgA + sub
                sl = sg % TC
                bm = bAm if sub == 0 else bBm
                bs = bAs if sub == 0 else bBs
                bank_prod(bm, dMs[sub])
                bank_prod(bs, dSs[sub])
                dEm = ve(lambda E, off=off, sl=sl, bm=bm: E.tensor_copy(
                    out=sacc[:, off + sl:off + sl + 1], in_=pb[bm][:, 0:1]),
                    deps=[dMs[sub]])
                bank_use(bm, dEm)
                dEs = ve(lambda E, off=off, sl=sl, bs=bs: E.tensor_copy(
                    out=sacc[:, off + TC + sl:off + TC + sl + 1], in_=pb[bs][:, 1:2]),
                    deps=[dSs[sub]])
                bank_use(bs, dEs)
                d_sl[sg] = casts
                if sl == TC - 1:
                    last_evac[0 if own else 1] = dEs

        rowsrc_r = Res()
        flush_tags = {}

        def flush_half(half_id):
            own = half_id == 0
            base = 0 if own else TC
            off = half_id * 2 * TC
            sl_m = slice(base, base + TC)
            ve(lambda E, sl_m=sl_m, off=off: E.tensor_scalar(
                out=gdm[:, sl_m], in0=sacc[:, off:off + TC], scalar1=-1.0 / D,
                scalar2=None, op0=ALU.mult), deps=[last_evac[half_id]])
            ve(lambda E, sl_m=sl_m: E.tensor_tensor(
                out=gr1[:, sl_m], in0=gdm[:, sl_m], in1=gdm[:, sl_m], op=ALU.mult))
            ve(lambda E, sl_m=sl_m, off=off: E.scalar_tensor_tensor(
                out=gsv[:, sl_m], in0=sacc[:, off + TC:off + 2 * TC], scalar=1.0 / D,
                in1=gr1[:, sl_m], op0=ALU.mult, op1=ALU.subtract))
            ve(lambda E, sl_m=sl_m: E.tensor_scalar(
                out=gsv[:, sl_m], in0=gsv[:, sl_m], scalar1=EPS_LN, scalar2=None,
                op0=ALU.add))
            dSq = sc(lambda E, sl_m=sl_m: E.sqrt(out=gsv[:, sl_m], in_=gsv[:, sl_m]),
                     deps=[("dv", P.cur("dv"))])
            dR1 = ve(lambda E, sl_m=sl_m: E.reciprocal(out=gr1[:, sl_m], in_=gsv[:, sl_m]),
                     deps=[dSq])
            dRD = ve(lambda E, sl_m=sl_m: E.tensor_tensor(
                out=grd[:, sl_m], in0=gr1[:, sl_m], in1=decf[:, sl_m], op=ALU.mult),
                deps=[d_dec])
            dRs1 = ve(lambda E, sl_m=sl_m: E.tensor_copy(
                out=rowsrc[:, :].rearrange("p (t s) -> p t s", s=2)[:, :, 0:1],
                in_=gdm[:, sl_m].rearrange("p (t o) -> p t o", o=1)),
                deps=rowsrc_r.users)
            rowsrc_r.users = []
            dRs2 = ve(lambda E, sl_m=sl_m: E.tensor_copy(
                out=rowsrc[:, :].rearrange("p (t s) -> p t s", s=2)[:, :, 1:2],
                in_=gsv[:, sl_m].rearrange("p (t o) -> p t o", o=1)))
            tb = 6 if own else 3
            bank_take(tb, deps=[dRs1, dRs2, ld_consts])
            dT = te(lambda E, tb=tb: E.transpose(out=pb[tb][0:2 * TC, 0:128],
                                                 in_=rowsrc[:, :], identity=idents[:, :]))
            bank_prod(tb, dT)
            dCt = ve(lambda E, tb=tb: E.tensor_copy(out=tacc[:, :],
                                                    in_=pb[tb][0:2 * TC, 0:128]),
                     deps=bank_dep(tb) + tacc_r.users)
            bank_use(tb, dCt)
            rowsrc_r.users.append(dT)
            srd = ap["sr0"] if own else ap["sr1"]
            ssem = "sa" if own else "sb"
            dSt = dma("sync", srd[:, :], tacc[:, :], ssem, deps=[dCt])
            tacc_r.users = [dSt]
            t0 = base * 128
            gsem = "gx0" if own else "gx1"
            dma("gpsimd", lhsT2[0:1, t0:t0 + TOWN].rearrange("a (t c) -> a t c", c=128),
                srd.rearrange("(t s) c -> s t c", s=2)[0:1, :, :], gsem, deps=[dSt])
            d_l2 = dma("gpsimd",
                       lhsT2[1:2, t0:t0 + TOWN].rearrange("a (t c) -> a t c", c=128),
                       srd.rearrange("(t s) c -> s t c", s=2)[1:2, :, :], gsem)
            d_rows = None
            if own:
                dma("sync", rowsMV[0:1, 0:TOWN].rearrange("a (t c) -> a t c", c=128),
                    srd.rearrange("(t s) c -> s t c", s=2)[0:1, :, :], "lx0",
                    deps=[dSt])
                d_rows = dma("sync",
                             rowsMV[0:1, TOWN:2 * TOWN].rearrange("a (t c) -> a t c", c=128),
                             srd.rearrange("(t s) c -> s t c", s=2)[1:2, :, :], "lx0")
            flush_tags[half_id] = (d_l2, dRD, dR1, d_rows)

        dKR = {}
        dVR = {}

        def emit_wsum(dt, bkk, bkv):
            p = dt % 2
            c0 = dt * DTILE
            bank_take(bkk, deps=[dW[dt][0], d_ones])
            for kc in range(KC):
                dm_ = te(lambda E, kc=kc, p=p, bkk=bkk: E.matmul(
                    out=pb[bkk][0:1, 0:DTILE], lhsT=onescol[:, :],
                    rhs=WkB[p][:, kc * DTILE:(kc + 1) * DTILE],
                    start=(kc == 0), stop=(kc == KC - 1)), inc=(kc == KC - 1))
            bank_prod(bkk, dm_)
            wkr[p].users.append(dm_)
            dE = ve(lambda E, bkk=bkk, c0=c0: E.tensor_copy(
                out=krhs2[0:1, c0:c0 + DTILE], in_=pb[bkk][0:1, 0:DTILE]),
                deps=bank_dep(bkk))
            bank_use(bkk, dE)
            dKR[dt] = dE
            bank_take(bkv, deps=[dW[dt][1], d_ones])
            for kc in range(KC):
                dm2 = te(lambda E, kc=kc, p=p, bkv=bkv: E.matmul(
                    out=pb[bkv][0:1, 0:DTILE], lhsT=onescol[:, :],
                    rhs=WvB[p][:, kc * DTILE:(kc + 1) * DTILE],
                    start=(kc == 0), stop=(kc == KC - 1)), inc=(kc == KC - 1))
            bank_prod(bkv, dm2)
            wvr[p].users.append(dm2)
            dE2 = ve(lambda E, bkv=bkv, c0=c0: E.tensor_copy(
                out=vrhs2[0:1, c0:c0 + DTILE], in_=pb[bkv][0:1, 0:DTILE]),
                deps=bank_dep(bkv))
            bank_use(bkv, dE2)
            dVR[dt] = dE2

        def xn_build_tt(tt, banks):
            d_rows = flush_tags[0][3]
            bcast_rows([(rowsMV[0:1, tt * NT:(tt + 1) * NT], Mb, d_rows, mb_r),
                        (rowsMV[0:1, TOWN + tt * NT:TOWN + (tt + 1) * NT], R1b,
                         d_rows, r1_r)], banks=banks)
            dRc = ve(lambda E: E.reciprocal(out=R1b[:, :], in_=R1b[:, :]),
                     deps=[r1_r.prod])
            r1_r.prod = dRc
            cast_deps = []
            for s in range(tt * (NT // 128), (tt + 1) * (NT // 128)):
                cast_deps += d_sl[s]
            for kc in range(KC):
                eng2 = gp
                d1 = eng2(lambda E, kc=kc, tt=tt: E.tensor_tensor(
                    out=xn_own[:, kc * TOWN + tt * NT:kc * TOWN + (tt + 1) * NT],
                    in0=xbf[:, kc * SEQ + tt * NT:kc * SEQ + (tt + 1) * NT],
                    in1=Mb[:, :], op=ALU.add),
                    deps=[mb_r.prod] + cast_deps)
                mb_r.users.append(d1)
                d2 = eng2(lambda E, kc=kc, tt=tt: E.tensor_tensor(
                    out=xn_own[:, kc * TOWN + tt * NT:kc * TOWN + (tt + 1) * NT],
                    in0=xn_own[:, kc * TOWN + tt * NT:kc * TOWN + (tt + 1) * NT],
                    in1=R1b[:, :], op=ALU.mult), deps=[r1_r.prod, d1])
                r1_r.users.append(d2)

        for ps in range(TC // 2):
            emit_slab2(ps, (1, 2, 0, 3))
        flush_half(0)
        load_kv_slices(1, extra=[("ac", P.cur("ac"))])
        emit_wsum(0, 2, 3)


        for dt in range(NDT):
            if dt == 1:
                es_sl.close()
                Wcol.extend(es_q2.enter_context(
                    nc.sbuf_tensor(f"Wcol{i}", [128, KC * 128], BF16, side="right"))
                    for i in range(3))
            if dt == min(2, NDT - 1):
                load_q(0)
                load_q(1)
                load_q(2)
            p = dt % 2
            c0 = dt * DTILE
            kvb = [4 + i for i in range(HPD)]

            pend = None

            def emit_kv(st):
                tc, kbuf, vbuf, dEK, dEV = st
                waits("tensor", [dEK, dEV])
                if tc == 0:
                    for i in kvb:
                        bank_take(i)
                for i in range(HPD):
                    dKV = te(lambda E, i=i, kbuf=kbuf, vbuf=vbuf, tc=tc, kvb=kvb: E.matmul(
                        out=pb[kvb[i]][:, 0:129],
                        lhsT=Ktile[kbuf][:, i * 128:(i + 1) * 128],
                        rhs=Vext[vbuf][:, i * 129:(i + 1) * 129],
                        start=(tc == 0), stop=(tc == 2 * TC - 1)),
                        inc=(i == HPD - 1))
                kt_r[kbuf].users.append(dKV)
                vx_r[vbuf].users.append(dKV)
                if tc == 2 * TC - 1:
                    for i in kvb:
                        bank_prod(i, dKV)

            for tc in range(2 * TC):
                own_side = tc < TC
                fl = flush_tags[0 if own_side else 1]
                kb = tc % 2
                vb = 2 + tc % 2
                kbuf = tc % 2
                vbuf = tc % 2
                bank_take(kb, deps=[dW[dt][0], d_bk, fl[0], dKR[dt]] + d_sl[tc])
                for kc in range(KC):
                    te(lambda E, kc=kc, kb=kb, tc=tc, p=p: E.matmul(
                        out=pb[kb][:, 0:DTILE],
                        lhsT=xbf[:, kc * SEQ + tc * 128:kc * SEQ + tc * 128 + 128],
                        rhs=WkB[p][:, kc * DTILE:(kc + 1) * DTILE],
                        start=(kc == 0), stop=False), inc=False)
                dPK = te(lambda E, kb=kb, c0=c0, tc=tc: E.matmul(
                    out=pb[kb][:, 0:DTILE], lhsT=lhsT2[:, tc * 128:(tc + 1) * 128],
                    rhs=krhs2[:, c0:c0 + DTILE], start=False, stop=True))
                bank_prod(kb, dPK)
                wkr[p].users.append(dPK)
                dEK = sc(lambda E, kb=kb, kbuf=kbuf, tc=tc: E.activation(
                    out=Ktile[kbuf][:, :], in_=pb[kb][:, 0:DTILE], func=AF.Relu,
                    scale=grd[:, tc:tc + 1]),
                    deps=kt_r[kbuf].users + bank_dep(kb) + [fl[1]])
                kt_r[kbuf].users = []
                bank_use(kb, dEK)
                bank_take(vb, deps=[dW[dt][1], d_bv, dVR[dt]] + d_sl[tc])
                for kc in range(KC):
                    te(lambda E, kc=kc, vb=vb, tc=tc, p=p: E.matmul(
                        out=pb[vb][:, 0:DTILE],
                        lhsT=xbf[:, kc * SEQ + tc * 128:kc * SEQ + tc * 128 + 128],
                        rhs=WvB[p][:, kc * DTILE:(kc + 1) * DTILE],
                        start=(kc == 0), stop=False), inc=False)
                dPV = te(lambda E, vb=vb, c0=c0, tc=tc: E.matmul(
                    out=pb[vb][:, 0:DTILE], lhsT=lhsT2[:, tc * 128:(tc + 1) * 128],
                    rhs=vrhs2[:, c0:c0 + DTILE], start=False, stop=True))
                bank_prod(vb, dPV)
                wvr[p].users.append(dPV)
                dEV = ve(lambda E, vb=vb, vbuf=vbuf, tc=tc: E.tensor_scalar(
                    out=Vext[vbuf][:, :].rearrange("p (h c) -> p h c", c=129)[:, :, 0:128],
                    in0=pb[vb][:, 0:DTILE].rearrange("p (h c) -> p h c", c=128),
                    scalar1=gr1[:, tc:tc + 1], scalar2=None, op0=ALU.mult),
                    deps=vx_r[vbuf].users + bank_dep(vb) + [fl[2]])
                vx_r[vbuf].users = []
                bank_use(vb, dEV)
                if pend is not None:
                    emit_kv(pend)
                pend = (tc, kbuf, vbuf, dEK, dEV)

                if dt == 0 and own_side:
                    if tc % 2 == 0:
                        emit_slab2((TC + tc) // 2, (kb, vb, 1 - kb, 5 - vb))
                    if tc == TC - 1:
                        flush_half(1)

            emit_kv(pend)
            if dt + 2 < NDT:
                load_kv_slices(dt + 2)
            if dt + 1 < NDT:
                emit_wsum(dt + 1, 0, 1)
            if dt < NTT:
                xn_build_tt(dt, (0, 1))
            for i in range(HPD):
                h = dt * HPD + i
                dE = ve(lambda E, i=i, h=h, kvb=kvb: E.tensor_copy(
                    out=KVbf[:, h * 129:(h + 1) * 129], in_=pb[kvb[i]][:, 0:129]),
                    deps=bank_dep(kvb[i]))
                bank_use(kvb[i], dE)

        barrier()
        if debug:
            dma("gpsimd", ap["d_vex"][:, :], Vext[(2 * TC - 1) % 2][:, :], "gx0")
            dma("gpsimd", ap["d_kt"][:, :], Ktile[(2 * TC - 1) % 2][:, :], "gx0")
            dma("sync", ap["d_sacc"][:, :], sacc[:, :], "st0")
            dma("sync", ap["d_gdm"][:, :], gdm[:, :], "st0")
            dma("sync", ap["d_gsv"][:, :], gsv[:, :], "st0")
            dma("gpsimd", ap["d_l2"][:, :], lhsT2[:, :], "gx0")
            barrier()
        es_xb.close()
        es3.close()
        es_pre2.close()

        es_g2 = ExitStack()
        xn2T = es_g2.enter_context(nc.sbuf_tensor("xn2T", [128, KC * TOWN], BF16))
        rows_o = es_g2.enter_context(nc.sbuf_tensor("rows_o", [1, TOWN], F32))

        ve(lambda E: E.tensor_tensor(
            out=rows_o[0:1, :], in0=rowsMV[0:1, TOWN:2 * TOWN],
            in1=rowsMV[0:1, TOWN:2 * TOWN], op=ALU.mult))
        ve(lambda E: E.reciprocal(out=rows_o[0:1, :], in_=rows_o[0:1, :]))
        ve(lambda E: E.tensor_scalar(
            out=rows_o[0:1, :], in0=rows_o[0:1, :],
            scalar1=-EPS_LN, scalar2=1.0 + EPS_LN, op0=ALU.mult, op1=ALU.add))
        ve(lambda E: E.reciprocal(out=rows_o[0:1, :], in_=rows_o[0:1, :]))
        es_out = ExitStack()
        outT = es_out.enter_context(nc.sbuf_tensor("outT", [128, KC * TOWN], BF16))

        es7a = ExitStack()
        WcolO = [es7a.enter_context(nc.sbuf_tensor(f"WcolO{i}", [128, KC * 128], BF16))
                 for i in range(3)]
        wo_r = [Res(), Res(), Res()]
        osem = ("ow0", "ow1", "ow2")
        dWo = {}

        def load_wo(do):
            b = do % 3
            dWo[do] = wslice_dma("woT", WcolO[b], do * 128, 128, osem[b], deps=wo_r[b].users)
            wo_r[b].users = []

        es6 = ExitStack()
        e6 = es6.enter_context
        WsB = [e6(nc.sbuf_tensor(f"WsB{i}", [128, KC * 128], F32)) for i in range(2)]
        wsr = [Res(), Res()]
        SgB = [e6(nc.sbuf_tensor(f"SgB{i}", [128, KC * 128], BF16)) for i in range(2)]
        sgr = [Res(), Res()]
        wsem = ("lw0", "lw1")
        dWs = {}

        def load_ws(do):
            pw = do % 2
            dWs[do] = wslice_dma("wsT", WsB[pw], do * 128, 128, wsem[pw],
                                 deps=wsr[pw].users, eng="sync")
            wsr[pw].users = []

        load_ws(0)

        esq = ExitStack()
        eq = esq.enter_context
        QT = eq(nc.sbuf_tensor("QT", [128, KC * TOWN], BF16, side="right"))
        for dq in range(KC):
            b = dq % 3
            for tt in range(NTT):
                bi = (dq * NTT + tt) % 4
                bank_take(bi, deps=[dWq[dq]])
                for kc in range(KC):
                    dP = te(lambda E, kc=kc, bi=bi, b=b, tt=tt: E.matmul(
                        out=pb[bi][:, 0:NT],
                        lhsT=Wcol[b][:, kc * 128:(kc + 1) * 128],
                        rhs=xn_own[:, kc * TOWN + tt * NT:kc * TOWN + (tt + 1) * NT],
                        start=(kc == 0), stop=(kc == KC - 1)), inc=(kc == KC - 1))
                bank_prod(bi, dP)
                wc_r[b].users.append(dP)
                dE = sc(lambda E, bi=bi, dq=dq, tt=tt: E.activation(
                    out=QT[:, dq * TOWN + tt * NT:dq * TOWN + (tt + 1) * NT],
                    in_=pb[bi][:, 0:NT], func=AF.Relu,
                    bias=grids["bq_g"][:, dq:dq + 1]), deps=bank_dep(bi))
                bank_use(bi, dE)
            if dq + 3 < KC:
                load_q(dq + 3)
        barrier()
        esq_q_done = True

        es4s = ExitStack()
        e4s = es4s.enter_context
        otile = [e4s(nc.sbuf_tensor(f"otile{i}", [128, 128], BF16)) for i in range(2)]
        ot_r = [Res(), Res()]
        R12b = e4s(nc.sbuf_tensor("R12b", [128, TOWN], F32))
        r12_r = Res()

        dRS = sc(lambda E: E.sqrt(out=rows_o[0:1, :], in_=rows_o[0:1, :]),
                 deps=[("dv", P.cur("dv"))])
        bcast_rows([(rows_o[0:1, :], R12b, dRS, r12_r)], banks=(6, 7))

        def emit_xn2_chunk(kc):
            eng2 = gp if kc % 2 == 0 else ve
            d1 = eng2(lambda E, kc=kc: E.tensor_tensor(
                out=xn2T[:, kc * TOWN:(kc + 1) * TOWN],
                in0=xn_own[:, kc * TOWN:(kc + 1) * TOWN], in1=R12b[:, :], op=ALU.mult),
                deps=([r12_r.prod] if r12_r.prod else []))
            eng2(lambda E, kc=kc: E.tensor_scalar(
                out=xn2T[:, kc * TOWN:(kc + 1) * TOWN],
                in0=xn2T[:, kc * TOWN:(kc + 1) * TOWN],
                scalar1=grids["lnw_g"][:, kc:kc + 1], scalar2=grids["lnb_g"][:, kc:kc + 1],
                op0=ALU.mult, op1=ALU.add))

        HH = H // 2
        nbk = (HH + 2) // 3

        def emit_num(v):
            tc, hf = v // 2, v % 2
            tloc = tc * 128
            base = nbk * hf
            for j in range(HH):
                h = hf * HH + j
                bi = base + j // 3
                col = (j % 3) * 129
                if j % 3 == 0:
                    bank_take(bi)
                dM = te(lambda E, h=h, bi=bi, col=col, tloc=tloc: E.matmul(
                    out=pb[bi][:, col:col + 129],
                    lhsT=QT[:, h * TOWN + tloc:h * TOWN + tloc + 128],
                    rhs=KVbf[:, h * 129:(h + 1) * 129],
                    start=True, stop=True))
                bank_prod(bi, dM)

        def emit_norm(v):
            tc, hf = v // 2, v % 2
            tloc = tc * 128
            base = nbk * hf
            h0g = hf * HH
            for bi in range(base, base + nbk):
                j0 = (bi - base) * 3
                nh = min(3, HH - j0)
                dS = ve(lambda E, bi=bi, j0=j0, nh=nh, tc=tc, h0g=h0g: E.tensor_scalar(
                    out=sden[:, h0g + j0:h0g + j0 + nh],
                    in0=pb[bi][:, 0:nh * 129].rearrange("p (h c) -> p h c", c=129)[:, :, 128:129],
                    scalar1=decf[:, tc:tc + 1], scalar2=EPS_ATT,
                    op0=ALU.mult, op1=ALU.add), deps=bank_dep(bi))
                bank_use(bi, dS)
            ve(lambda E, h0g=h0g: E.reciprocal(out=stok[:, h0g:h0g + HH],
                                               in_=sden[:, h0g:h0g + HH]))
            dZ = ve(lambda E, tc=tc, h0g=h0g: E.tensor_scalar(
                out=stok[:, h0g:h0g + HH], in0=stok[:, h0g:h0g + HH],
                scalar1=decf[:, tc:tc + 1], scalar2=None, op0=ALU.mult))
            for j in range(HH):
                h = h0g + j
                bi = base + j // 3
                col = (j % 3) * 129
                ob = j % 2
                dO = ve(lambda E, bi=bi, col=col, h=h, ob=ob: E.tensor_scalar(
                    out=otile[ob][:, :], in0=pb[bi][:, col:col + 128],
                    scalar1=stok[:, h:h + 1], scalar2=None, op0=ALU.mult),
                    deps=ot_r[ob].users + [dZ])
                ot_r[ob].users = []
                bank_use(bi, dO)
                tb = 6 + ob
                bank_take(tb, deps=[dO, d_idb])
                dT = te(lambda E, ob=ob, tb=tb: E.matmul(
                    out=pb[tb][:, 0:128], lhsT=otile[ob][:, :], rhs=identb[:, :],
                    start=True, stop=True))
                bank_prod(tb, dT)
                ot_r[ob].users.append(dT)
                dC = sc(lambda E, h=h, tb=tb, tloc=tloc: E.copy(
                    out=outT[:, h * TOWN + tloc:h * TOWN + tloc + 128],
                    in_=pb[tb][:, 0:128]), deps=bank_dep(tb))
                bank_use(tb, dC)
            emit_xn2_chunk(v)

        prev = None
        for v in range(2 * TC):
            emit_num(v)
            if prev is not None:
                emit_norm(prev)
            prev = v
        emit_norm(prev)
        d_xn2 = ("gp", P.cur("gp"))
        d_xn2v = ("dv", P.cur("dv"))
        barrier()
        es4s.close()
        esq.close()
        es_g = ExitStack()
        gRaw = es_g.enter_context(nc.sbuf_tensor("gRaw", [128, KC * TOWN], BF16,
                                                 side="right"))

        load_wo(0)
        load_wo(1)
        load_wo(2)
        for do in range(KC):
            if do + 1 < KC:
                load_ws(do + 1)
            pw = do % 2
            dSg = sc(lambda E, pw=pw: E.sign(out=SgB[pw][:, :], in_=WsB[pw][:, :]),
                     deps=sgr[pw].users + [dWs[do]])
            sgr[pw].users = []
            dMx = ve(lambda E, do=do, pw=pw: E.reduce_max(
                out=maxg[:, do:do + 1], in_=WsB[pw][:, :], axis=AX.X,
                apply_absolute_value=True), deps=[dWs[do]])
            wsr[pw].users += [dSg, dMx]
            for tt in range(NTT):
                bi = (do * NTT + tt) % 4
                bank_take(bi, deps=[dSg, d_xn2, d_xn2v])
                for kc in range(KC):
                    dP = te(lambda E, kc=kc, bi=bi, tt=tt, pw=pw: E.matmul(
                        out=pb[bi][:, 0:NT],
                        lhsT=SgB[pw][:, kc * 128:(kc + 1) * 128],
                        rhs=xn2T[:, kc * TOWN + tt * NT:kc * TOWN + (tt + 1) * NT],
                        start=(kc == 0), stop=(kc == KC - 1)), inc=(kc == KC - 1))
                bank_prod(bi, dP)
                sgr[pw].users.append(dP)
                dE = ve(lambda E, bi=bi, do=do, tt=tt: E.tensor_copy(
                    out=gRaw[:, do * TOWN + tt * NT:do * TOWN + (tt + 1) * NT],
                    in_=pb[bi][:, 0:NT]), deps=bank_dep(bi))
                bank_use(bi, dE)
        ve(lambda E: E.reduce_max(out=maxg[:, KC:KC + 1], in_=maxg[:, 0:KC], axis=AX.X))
        bank_take(6, deps=[("dv", P.cur("dv"))])
        dT = te(lambda E: E.transpose(out=pb[6][0:1, 0:128], in_=maxg[:, KC:KC + 1],
                                      identity=idents[:, :]))
        bank_prod(6, dT)
        dC = ve(lambda E: E.tensor_copy(out=s11r[0:1, 0:128], in_=pb[6][0:1, 0:128]),
                deps=bank_dep(6))
        bank_use(6, dC)
        dM2 = ve(lambda E: E.reduce_max(out=s11r[0:1, 128:129], in_=s11r[0:1, 0:128],
                                        axis=AX.X))
        bank_take(7, deps=[dM2])
        dB = te(lambda E: E.matmul(out=pb[7][:, 0:1], lhsT=ones1f[:, :],
                                   rhs=s11r[0:1, 128:129], start=True, stop=True))
        bank_prod(7, dB)
        dC2 = ve(lambda E: E.tensor_copy(out=s11c[:, :], in_=pb[7][:, 0:1]),
                 deps=bank_dep(7))
        bank_use(7, dC2)
        ve(lambda E: E.tensor_scalar(
            out=gsc_g[:, :], in0=grids["gam_g"][:, :], scalar1=s11c[:, :], scalar2=None,
            op0=ALU.mult))
        barrier()
        es6.close()

        es7b = ExitStack()
        e7 = es7b.enter_context
        outF = [e7(nc.sbuf_tensor(f"outF{i}", [128, TOWN], F32)) for i in range(2)]
        of_r = [Res(), Res()]
        xres = [e7(nc.sbuf_tensor(f"xres{i}", [128, TOWN], F32)) for i in range(2)]
        xr_r = [Res(), Res()]
        dsil = {}
        for do in range(KC):
            for tt in range(NTT):
                dsil[(do, tt)] = sc(lambda E, do=do, tt=tt: E.activation(
                    out=gRaw[:, do * TOWN + tt * NT:do * TOWN + (tt + 1) * NT],
                    in_=gRaw[:, do * TOWN + tt * NT:do * TOWN + (tt + 1) * NT],
                    func=AF.Silu, scale=gsc_g[:, do:do + 1],
                    bias=grids["bet_g"][:, do:do + 1]))
        for do in range(KC):
            buf = do % 2
            b3 = do % 3
            dX = dma("scalar", xres[buf][:, :], ap["xT"][do * 128:(do + 1) * 128, :],
                     "lr0" if buf == 0 else "lr1", deps=xr_r[buf].users)
            xr_r[buf].users = []
            for tt in range(NTT):
                bi = (do * NTT + tt) % 4
                bank_take(bi, deps=[dWo[do]])
                for kc in range(KC):
                    dP = te(lambda E, kc=kc, bi=bi, b3=b3, tt=tt: E.matmul(
                        out=pb[bi][:, 0:NT],
                        lhsT=WcolO[b3][:, kc * 128:(kc + 1) * 128],
                        rhs=outT[:, kc * TOWN + tt * NT:kc * TOWN + (tt + 1) * NT],
                        start=(kc == 0), stop=(kc == KC - 1)), inc=(kc == KC - 1))
                bank_prod(bi, dP)
                wo_r[b3].users.append(dP)
                d1 = ve(lambda E, buf=buf, tt=tt, do=do, bi=bi: E.scalar_tensor_tensor(
                    out=outF[buf][:, tt * NT:(tt + 1) * NT], in0=pb[bi][:, 0:NT],
                    scalar=grids["bo_g"][:, do:do + 1],
                    in1=gRaw[:, do * TOWN + tt * NT:do * TOWN + (tt + 1) * NT],
                    op0=ALU.add, op1=ALU.mult),
                    deps=of_r[buf].users + [dsil[(do, tt)]] + bank_dep(bi))
                of_r[buf].users = []
                bank_use(bi, d1)
                d2 = ve(lambda E, buf=buf, tt=tt: E.tensor_tensor(
                    out=outF[buf][:, tt * NT:(tt + 1) * NT],
                    in0=outF[buf][:, tt * NT:(tt + 1) * NT],
                    in1=xres[buf][:, tt * NT:(tt + 1) * NT], op=ALU.add), deps=[dX])
                xr_r[buf].users.append(d2)
                of_r[buf].users.append(d2)
            dSt = dma("sync", ap["yT"][do * 128:(do + 1) * 128, :], outF[buf][:, :],
                      "st0" if buf == 0 else "st1", deps=of_r[buf].users)
            of_r[buf].users.append(dSt)
            if do + 3 < KC:
                load_wo(do + 3)

        if debug:
            barrier()
            dbg_g0 = P.cur("gx0")
            dma("gpsimd", ap["d_xn"][:, :], xn_own[:, :], "gx0")
            dma("gpsimd", ap["d_kv"][:, :], KVbf[:, :], "gx0")
            dma("gpsimd", ap["d_xn2"][:, :], xn2T[:, :], "gx0")
            dma("gpsimd", ap["d_out"][:, :], outT[:, :], "gx0")
            dma("gpsimd", ap["d_graw"][:, :], gRaw[:, :], "gx0")
        W("sync", "st0", P.cur("st0"))
        W("sync", "st1", P.cur("st1"))
        if debug:
            W("sync", "gx0", P.cur("gx0"))
        es7b.close()
        es_g.close()
        es_q2.close()
        es7a.close()
        es_out.close()
        es_g2.close()
        es_pre.close()

        with nc.Block() as block:
            @block.sync
            def _(E):
                for fn in P.ops["sync"]:
                    fn(E)

            @block.gpsimd
            def _(E):
                for fn in P.ops["gpsimd"]:
                    fn(E)

            @block.vector
            def _(E):
                for fn in P.ops["vector"]:
                    fn(E)

            @block.scalar
            def _(E):
                for fn in P.ops["scalar"]:
                    fn(E)

            @block.tensor
            def _(E):
                for fn in P.ops["tensor"]:
                    fn(E)

    return nc



def stage_inputs(x, Wq, bq, Wk, bk, Wv, bv, Wo, bo, Ws, gam, bet, lnw, lnb,
                 D, TOWN, n_cores):
    KC = D // 128
    TC = TOWN // 128
    c = np.ascontiguousarray
    shared = {
        "wqT": c(Wq.T), "wkT": c(Wk.T), "wvT": c(Wv.T), "woT": c(Wo.T), "wsT": c(Ws.T),
        "bq_g": c(bq.reshape(KC, 128).T), "bo_g": c(bo.reshape(KC, 128).T),
        "lnw_g": c(lnw.reshape(KC, 128).T), "lnb_g": c(lnb.reshape(KC, 128).T),
        "gam_g": c(gam.reshape(KC, 128).T), "bet_g": c(bet.reshape(KC, 128).T),
        "bk_r": c(bk.reshape(1, D)), "bv_r": c(bv.reshape(1, D)),
        "ident": np.eye(128, dtype=np.float32),
    }
    in_maps = []
    for core in range(n_cores):
        b, half = core // 2, core % 2
        t0, t1 = half * TOWN, (1 - half) * TOWN
        pos = np.zeros((128, 2 * TC), dtype=np.float32)
        for i in range(TC):
            pos[:, i] = t0 + i * 128 + np.arange(128)
            pos[:, TC + i] = t1 + i * 128 + np.arange(128)
        m = dict(shared)
        m["xT"] = c(x[b, t0:t0 + TOWN, :].T)
        m["xoT"] = c(x[b, t1:t1 + TOWN, :].T)
        m["posf_g"] = pos
        in_maps.append(m)
    return in_maps


def kernel(L0_attn_x, L0_attn_Wq, L0_attn_bq, L0_attn_Wk, L0_attn_bk,
           L0_attn_Wv, L0_attn_bv, L0_attn_Wo, L0_attn_bo,
           L0_ffn_weight, L0_ffn_gamma, L0_ffn_beta, L0_ffn_ln_weight, L0_ffn_ln_bias):
    D, TOWN = 2048, 1024
    f = lambda a: np.asarray(a, dtype=np.float32)
    x = f(L0_attn_x)
    in_maps = stage_inputs(
        x, f(L0_attn_Wq), f(L0_attn_bq), f(L0_attn_Wk), f(L0_attn_bk),
        f(L0_attn_Wv), f(L0_attn_bv), f(L0_attn_Wo), f(L0_attn_bo),
        f(L0_ffn_weight), f(L0_ffn_gamma), f(L0_ffn_beta),
        f(L0_ffn_ln_weight), f(L0_ffn_ln_bias), D=D, TOWN=TOWN, n_cores=N_CORES)
    nc = build_program(D=D, TOWN=TOWN)
    res = run_bass_kernel_spmd(nc, in_maps, core_ids=list(range(N_CORES)))
    B, S = L0_attn_x.shape[0], L0_attn_x.shape[1]
    out = np.empty((B, S, D), dtype=np.float32)
    for core in range(N_CORES):
        b, half = core // 2, core % 2
        t0 = half * TOWN
        out[b, t0:t0 + TOWN, :] = np.asarray(res.results[core]["yT"]).T
    return out


# version 36 (final)
# speedup vs baseline: 1.0032x; 1.0032x over previous
import numpy as np
from contextlib import ExitStack

import concourse.bass as bass
import concourse.mybir as mybir
from concourse.bass_utils import run_bass_kernel_spmd

F32 = mybir.dt.float32
BF16 = mybir.dt.bfloat16
AF = mybir.ActivationFunctionType
ALU = mybir.AluOpType
AX = mybir.AxisListType

N_CORES = 8
EPS_LN = 1e-5
EPS_ATT = 1e-6


class Emitter:
    ENGS = ("sync", "gpsimd", "vector", "scalar", "tensor")

    def __init__(self):
        self.ops = {e: [] for e in self.ENGS}
        self.cnt = {}

    def op(self, eng, fn):
        self.ops[eng].append(fn)

    def bump(self, sem, k):
        self.cnt[sem] = self.cnt.get(sem, 0) + k
        return self.cnt[sem]

    def cur(self, sem):
        return self.cnt.get(sem, 0)


class Res:

    def __init__(self):
        self.prod = None
        self.users = []


def build_program(D=2048, TOWN=1024, DTILE=512, debug=False):
    KC = D // 128
    H = D // 128
    TC = TOWN // 128
    SEQ = 2 * TOWN
    NT = min(512, TOWN)
    NTT = TOWN // NT
    NDT = D // DTILE
    HPD = DTILE // 128

    nc = bass.Bass("TRN2", target_bir_lowering=False)

    ap = {}

    def dram(name, shape, dt=F32, kind="ExternalInput"):
        ap[name] = nc.dram_tensor(name, list(shape), dt, kind=kind).ap()

    dram("xT", (D, TOWN))
    dram("xoT", (D, TOWN))
    for w in ("wqT", "wkT", "wvT", "woT", "wsT"):
        dram(w, (D, D))
    for g in ("bq_g", "bo_g", "lnw_g", "lnb_g", "gam_g", "bet_g"):
        dram(g, (128, KC))
    dram("bk_r", (1, D))
    dram("bv_r", (1, D))
    dram("posf_g", (128, 2 * TC))
    dram("ident", (128, 128))
    dram("sr0", (2 * TOWN // 128, 128), F32, kind="Internal")
    dram("sr1", (2 * TOWN // 128, 128), F32, kind="Internal")
    dram("yT", (D, TOWN), kind="ExternalOutput")
    if debug:
        dram("d_sacc", (128, 4 * TC), kind="ExternalOutput")
        dram("d_gdm", (128, 2 * TC), kind="ExternalOutput")
        dram("d_gsv", (128, 2 * TC), kind="ExternalOutput")
        dram("d_l2", (2, SEQ), kind="ExternalOutput")
        dram("d_xn", (128, KC * TOWN), kind="ExternalOutput")
        dram("d_kv", (128, (D // 128) * 129), kind="ExternalOutput")
        dram("d_xn2", (128, KC * TOWN), kind="ExternalOutput")
        dram("d_out", (128, KC * TOWN), kind="ExternalOutput")
        dram("d_graw", (128, KC * TOWN), kind="ExternalOutput")
        dram("d_vex", (128, (DTILE // 128) * 129), kind="ExternalOutput")
        dram("d_kt", (128, DTILE), kind="ExternalOutput")

    P = Emitter()
    SEMS = ("ld", "lt0", "lt1", "lx0", "lx1", "lw0", "lw1", "lr0", "lr1",
            "wx", "wl0", "wl1", "wl2", "wl3", "wl4", "gx0", "gx1", "qw0",
            "qw1", "qw2", "ow0", "ow1", "ow2", "sa", "sb",
            "pe", "dv", "ac", "gp", "st0", "st1")
    DMA_SEMS = ("ld", "lt0", "lt1", "lx0", "lx1", "lw0", "lw1", "lr0", "lr1",
                "wx", "wl0", "wl1", "wl2", "wl3", "wl4", "gx0", "gx1", "qw0",
                "qw1", "qw2", "ow0", "ow1", "ow2", "sa", "sb",
                "st0", "st1")

    with ExitStack() as es0:
        e0 = es0.enter_context
        sem = {s: e0(nc.semaphore(s)) for s in SEMS}

        rowsMV = e0(nc.sbuf_tensor("rowsMV", [1, 2 * TOWN], F32))
        grids = {}
        for g in ("bq_g", "bo_g", "lnw_g", "lnb_g", "gam_g", "bet_g", "posf_g"):
            w = 2 * TC if g == "posf_g" else KC
            grids[g] = e0(nc.sbuf_tensor(g + "_s", [128, w], F32))
        decf = e0(nc.sbuf_tensor("decf", [128, 2 * TC], F32))
        idents = e0(nc.sbuf_tensor("idents", [128, 128], F32))
        identb = e0(nc.sbuf_tensor("identb", [128, 128], BF16))
        ones1f = e0(nc.sbuf_tensor("ones1f", [1, 128], F32))
        KVbf = e0(nc.sbuf_tensor("KVbf", [128, H * 129], BF16))
        sden = e0(nc.sbuf_tensor("sden", [128, H], F32))
        stok = e0(nc.sbuf_tensor("stok", [128, H], F32))
        gsc_g = e0(nc.sbuf_tensor("gsc_g", [128, KC], F32))
        s11c = e0(nc.sbuf_tensor("s11c", [128, 1], F32))
        maxg = e0(nc.sbuf_tensor("maxg", [128, KC + 1], F32))
        s11r = e0(nc.sbuf_tensor("s11r", [1, 130], F32))

        pb = [e0(nc.psum_tensor(f"pb{i}", [128, 512], F32)) for i in range(8)]
        bank = [Res() for _ in range(8)]

        def W(eng, s, n):
            if n > 0:
                P.op(eng, lambda E, s=s, n=n: E.wait_ge(sem[s], n))

        def waits(eng, deps):
            need = {}
            for s, n in deps:
                if s in DMA_SEMS:
                    n = P.cur(s)
                if n > need.get(s, -1):
                    need[s] = n
            for s, n in need.items():
                W(eng, s, n)

        def ve(fn, deps=()):
            waits("vector", deps)
            W("vector", "dv", P.cur("dv"))
            P.op("vector", lambda E, fn=fn: fn(E).then_inc(sem["dv"], 1))
            return ("dv", P.bump("dv", 1))

        def gp(fn, deps=()):
            waits("gpsimd", deps)
            W("gpsimd", "gp", P.cur("gp"))
            P.op("gpsimd", lambda E, fn=fn: fn(E).then_inc(sem["gp"], 1))
            return ("gp", P.bump("gp", 1))

        def sc(fn, deps=()):
            waits("scalar", deps)
            W("scalar", "ac", P.cur("ac"))
            P.op("scalar", lambda E, fn=fn: fn(E).then_inc(sem["ac"], 1))
            return ("ac", P.bump("ac", 1))

        def te(fn, deps=(), inc=True):
            waits("tensor", deps)
            if inc:
                P.op("tensor", lambda E, fn=fn: fn(E).then_inc(sem["pe"], 1))
                return ("pe", P.bump("pe", 1))
            P.op("tensor", lambda E, fn=fn: fn(E))
            return ("pe", P.cur("pe") + 1)

        def dma(eng, out, in_, s_name, deps=()):
            waits(eng, deps)
            P.op(eng, lambda E, out=out, in_=in_: E.dma_start(out=out, in_=in_).then_inc(sem[s_name], 16))
            return (s_name, P.bump(s_name, 16))

        def wslice_dma(wname, dst, c0, width, s_name, deps=(), eng="gpsimd", nsplit=4):
            waits(eng, deps)
            last = None
            step = KC // nsplit
            for j in range(nsplit):
                src = ap[wname].rearrange("(kc p) c -> p kc c", p=128)[
                    :, j * step:(j + 1) * step, c0:c0 + width]
                d = dst[:, 0:KC * width].rearrange("p (kc c) -> p kc c", c=width)[
                    :, j * step:(j + 1) * step, :]
                P.op(eng, lambda E, d=d, src=src: E.dma_start(out=d, in_=src).then_inc(sem[s_name], 16))
                last = (s_name, P.bump(s_name, 16))
            return last

        def bank_take(i, deps=()):
            waits("tensor", bank[i].users + ([bank[i].prod] if bank[i].prod else []) + list(deps))
            bank[i].users = []

        def bank_prod(i, tag):
            bank[i].prod = tag

        def bank_dep(i):
            return [bank[i].prod] if bank[i].prod else []

        def bank_use(i, tag):
            bank[i].users.append(tag)

        def barrier():
            tgt = {s: P.cur(s) for s in SEMS}
            for eng in P.ENGS:
                for s, n in tgt.items():
                    W(eng, s, n)
            for b in bank:
                b.prod = None
                b.users = []

        def bcast_rows(pairs, banks):
            jj = 0
            for (srow, dst_b, dep, dst_r) in pairs:
                w = srow.shape[-1]
                first = True
                for c0 in range(0, w, 512):
                    cw = min(512, w - c0)
                    bi = banks[jj % 2]
                    jj += 1
                    bank_take(bi, deps=([dep] if dep else []) + [("dv", P.cur("dv"))])
                    dM = te(lambda E, srow=srow, bi=bi, c0=c0, cw=cw: E.matmul(
                        out=pb[bi][:, 0:cw], lhsT=ones1f[:, :], rhs=srow[:, c0:c0 + cw],
                        start=True, stop=True))
                    bank_prod(bi, dM)
                    dC = ve(lambda E, dst_b=dst_b, bi=bi, c0=c0, cw=cw: E.tensor_copy(
                        out=dst_b[:, c0:c0 + cw], in_=pb[bi][:, 0:cw]),
                        deps=bank_dep(bi) + (dst_r.users if first else []))
                    first = False
                    bank_use(bi, dC)
                dst_r.users = []
                dst_r.prod = dC

        for g in ("bq_g", "bo_g", "lnw_g", "lnb_g", "gam_g", "bet_g", "posf_g"):
            dma("sync", grids[g][:, :], ap[g][:, :], "ld")
        dma("sync", idents[:, :], ap["ident"][:, :], "ld")
        ld_consts = ("ld", P.cur("ld"))

        ve(lambda E: E.memset(ones1f[:, :], 1.0))
        d_idb = ve(lambda E: E.tensor_copy(out=identb[:, :], in_=idents[:, :]), deps=[ld_consts])
        d_dec = sc(lambda E: E.activation(out=decf[:, :], in_=grids["posf_g"][:, :],
                                          func=AF.Exp, scale=-1.0 / SEQ), deps=[ld_consts])

        HBW = D // 2
        KC2 = KC // 2
        es_pre = ExitStack()
        xn_own = es_pre.enter_context(nc.sbuf_tensor("xn_own", [128, KC * TOWN], BF16))
        es_pre2 = ExitStack()
        epre = es_pre2.enter_context
        tacc = epre(nc.sbuf_tensor("tacc", [2 * TC, 128], F32))
        tacc_r = Res()
        sacc = epre(nc.sbuf_tensor("sacc", [128, 4 * TC], F32))
        gdm = epre(nc.sbuf_tensor("gdm", [128, 2 * TC], F32))
        gsv = epre(nc.sbuf_tensor("gsv", [128, 2 * TC], F32))
        gr1 = epre(nc.sbuf_tensor("gr1", [128, 2 * TC], F32))
        grd = epre(nc.sbuf_tensor("grd", [128, 2 * TC], F32))
        rowsrc = epre(nc.sbuf_tensor("rowsrc", [128, 2 * TC], F32))
        lhsT2 = epre(nc.sbuf_tensor("lhsT2", [2, SEQ], BF16))
        onescol = epre(nc.sbuf_tensor("onescol", [128, 1], BF16))
        Mb = epre(nc.sbuf_tensor("Mb", [128, NT], F32))
        mb_r = Res()
        R1b = epre(nc.sbuf_tensor("R1b", [128, NT], F32))
        r1_r = Res()

        es3 = ExitStack()
        e3 = es3.enter_context
        WkB = [e3(nc.sbuf_tensor(f"WkB{i}", [128, KC * DTILE], BF16)) for i in range(2)]
        WvB = [e3(nc.sbuf_tensor(f"WvB{i}", [128, KC * DTILE], BF16)) for i in range(2)]
        wkr = [Res(), Res()]
        wvr = [Res(), Res()]
        Ktile = [e3(nc.sbuf_tensor(f"Ktile{i}", [128, DTILE], BF16)) for i in range(2)]
        kt_r = [Res(), Res()]
        Vext = [e3(nc.sbuf_tensor(f"Vext{i}", [128, HPD * 129], BF16)) for i in range(2)]
        vx_r = [Res(), Res()]
        krhs2 = e3(nc.sbuf_tensor("krhs2", [2, D], BF16))
        vrhs2 = e3(nc.sbuf_tensor("vrhs2", [2, D], BF16))
        d_bk = dma("gpsimd", krhs2[1:2, :], ap["bk_r"][:, :], "wx")
        d_bv = dma("gpsimd", vrhs2[1:2, :], ap["bv_r"][:, :], "wx")
        for i in range(2):
            ve(lambda E, i=i: E.memset(
                Vext[i][:, :].rearrange("p (h c) -> p h c", c=129)[:, :, 128:129], 1.0))
        d_ones = ve(lambda E: E.memset(onescol[:, :], 1.0))

        es_xb = ExitStack()
        xbf = es_xb.enter_context(nc.sbuf_tensor("xbf", [128, KC * SEQ], BF16))
        es_sl = ExitStack()
        xf = [es_sl.enter_context(nc.sbuf_tensor(f"xf{i}", [128, D // 2], F32))
              for i in range(2)]
        xf_r = [Res(), Res()]
        xsq = [es_sl.enter_context(nc.sbuf_tensor(f"xsq{i}", [128, D // 2], BF16))
               for i in range(2)]

        ksem = ("wl0", "wl1")
        vsem = ("wl2", "wl3")
        dW = {}

        def load_kv_slices(dt, extra=()):
            p = dt % 2
            dWk = wslice_dma("wkT", WkB[p], dt * DTILE, DTILE, ksem[p],
                             deps=wkr[p].users + list(extra))
            wkr[p].users = []
            dWv = wslice_dma("wvT", WvB[p], dt * DTILE, DTILE, vsem[p], deps=wvr[p].users)
            wvr[p].users = []
            dW[dt] = (dWk, dWv)

        load_kv_slices(0)

        wc_r = [Res(), Res(), Res()]
        qsem = ("qw0", "qw1", "qw2")
        dWq = {}
        Wcol = []
        es_q2 = ExitStack()

        def load_q(dq):
            b = dq % 3
            dWq[dq] = wslice_dma("wqT", Wcol[b], dq * 128, 128, qsem[b], deps=wc_r[b].users)
            wc_r[b].users = []

        NQ = 4
        KCq = KC // NQ
        d_sl = {}
        last_evac = {}
        sq_r = [Res(), Res()]

        def emit_slab2(ps, banks4):
            bAm, bAs, bBm, bBs = banks4
            sgA = 2 * ps
            own = sgA < TC
            src = "xT" if own else "xoT"
            pl = (sgA % TC) // 2
            scolA = sgA * 128
            off = 0 if own else 2 * TC
            casts = []
            for b in banks4:
                bank_take(b)
            dMs = {0: None, 1: None}
            dSs = {0: None, 1: None}
            for q in range(NQ):
                p = q % 2
                eng_d, semn = ("sync", "lt0") if p == 0 else ("scalar", "lt1")
                dL = dma(eng_d, xf[p][:, :].rearrange("p (kc t) -> p kc t", t=256),
                         ap[src].rearrange("(kc p) t -> p kc t", p=128)[
                             :, q * KCq:(q + 1) * KCq, pl * 256:(pl + 1) * 256],
                         semn, deps=xf_r[p].users)
                xf_r[p].users = []
                dC = sc(lambda E, p=p, q=q, scolA=scolA: E.copy(
                    out=xbf[:, :].rearrange("p (kc t) -> p kc t", t=SEQ)[
                        :, q * KCq:(q + 1) * KCq, scolA:scolA + 256],
                    in_=xf[p][:, :].rearrange("p (kc t) -> p kc t", t=256)),
                    deps=[dL])
                dQ = sc(lambda E, p=p: E.activation(
                    out=xsq[p][:, :], in_=xf[p][:, :], func=AF.Square),
                    deps=sq_r[p].users)
                sq_r[p].users = []
                xf_r[p].users.append(dQ)
                casts.append(dC)
                for k2 in range(KCq):
                    kc = q * KCq + k2
                    for sub in range(2):
                        sg = sgA + sub
                        bm = bAm if sub == 0 else bBm
                        bs = bAs if sub == 0 else bBs
                        col = k2 * 256 + sub * 128
                        dMs[sub] = te(lambda E, kc=kc, sg=sg, bm=bm: E.matmul(
                            out=pb[bm][:, 0:1],
                            lhsT=xbf[:, kc * SEQ + sg * 128:kc * SEQ + sg * 128 + 128],
                            rhs=onescol[:, :], start=(kc == 0), stop=(kc == KC - 1)),
                            inc=(kc == KC - 1),
                            deps=([dC, d_ones] if k2 == 0 else []))
                        dSs[sub] = te(lambda E, kc=kc, col=col, p=p, bs=bs: E.matmul(
                            out=pb[bs][:, 1:2],
                            lhsT=xsq[p][:, col:col + 128],
                            rhs=onescol[:, :], start=(kc == 0), stop=(kc == KC - 1)),
                            inc=True,
                            deps=([dQ] if k2 == 0 else []))
                sq_r[p].users.append(dSs[1])
            for sub in range(2):
                sg = sgA + sub
                sl = sg % TC
                bm = bAm if sub == 0 else bBm
                bs = bAs if sub == 0 else bBs
                bank_prod(bm, dMs[sub])
                bank_prod(bs, dSs[sub])
                dEm = ve(lambda E, off=off, sl=sl, bm=bm: E.tensor_copy(
                    out=sacc[:, off + sl:off + sl + 1], in_=pb[bm][:, 0:1]),
                    deps=[dMs[sub]])
                bank_use(bm, dEm)
                dEs = ve(lambda E, off=off, sl=sl, bs=bs: E.tensor_copy(
                    out=sacc[:, off + TC + sl:off + TC + sl + 1], in_=pb[bs][:, 1:2]),
                    deps=[dSs[sub]])
                bank_use(bs, dEs)
                d_sl[sg] = casts
                if sl == TC - 1:
                    last_evac[0 if own else 1] = dEs

        rowsrc_r = Res()
        flush_tags = {}

        def flush_half(half_id):
            own = half_id == 0
            base = 0 if own else TC
            off = half_id * 2 * TC
            sl_m = slice(base, base + TC)
            ve(lambda E, sl_m=sl_m, off=off: E.tensor_scalar(
                out=gdm[:, sl_m], in0=sacc[:, off:off + TC], scalar1=-1.0 / D,
                scalar2=None, op0=ALU.mult), deps=[last_evac[half_id]])
            ve(lambda E, sl_m=sl_m: E.tensor_tensor(
                out=gr1[:, sl_m], in0=gdm[:, sl_m], in1=gdm[:, sl_m], op=ALU.mult))
            ve(lambda E, sl_m=sl_m, off=off: E.scalar_tensor_tensor(
                out=gsv[:, sl_m], in0=sacc[:, off + TC:off + 2 * TC], scalar=1.0 / D,
                in1=gr1[:, sl_m], op0=ALU.mult, op1=ALU.subtract))
            ve(lambda E, sl_m=sl_m: E.tensor_scalar(
                out=gsv[:, sl_m], in0=gsv[:, sl_m], scalar1=EPS_LN, scalar2=None,
                op0=ALU.add))
            dSq = sc(lambda E, sl_m=sl_m: E.sqrt(out=gsv[:, sl_m], in_=gsv[:, sl_m]),
                     deps=[("dv", P.cur("dv"))])
            dR1 = ve(lambda E, sl_m=sl_m: E.reciprocal(out=gr1[:, sl_m], in_=gsv[:, sl_m]),
                     deps=[dSq])
            dRD = ve(lambda E, sl_m=sl_m: E.tensor_tensor(
                out=grd[:, sl_m], in0=gr1[:, sl_m], in1=decf[:, sl_m], op=ALU.mult),
                deps=[d_dec])
            dRs1 = ve(lambda E, sl_m=sl_m: E.tensor_copy(
                out=rowsrc[:, :].rearrange("p (t s) -> p t s", s=2)[:, :, 0:1],
                in_=gdm[:, sl_m].rearrange("p (t o) -> p t o", o=1)),
                deps=rowsrc_r.users)
            rowsrc_r.users = []
            dRs2 = ve(lambda E, sl_m=sl_m: E.tensor_copy(
                out=rowsrc[:, :].rearrange("p (t s) -> p t s", s=2)[:, :, 1:2],
                in_=gsv[:, sl_m].rearrange("p (t o) -> p t o", o=1)))
            tb = 6 if own else 3
            bank_take(tb, deps=[dRs1, dRs2, ld_consts])
            dT = te(lambda E, tb=tb: E.transpose(out=pb[tb][0:2 * TC, 0:128],
                                                 in_=rowsrc[:, :], identity=idents[:, :]))
            bank_prod(tb, dT)
            dCt = ve(lambda E, tb=tb: E.tensor_copy(out=tacc[:, :],
                                                    in_=pb[tb][0:2 * TC, 0:128]),
                     deps=bank_dep(tb) + tacc_r.users)
            bank_use(tb, dCt)
            rowsrc_r.users.append(dT)
            srd = ap["sr0"] if own else ap["sr1"]
            ssem = "sa" if own else "sb"
            dSt = dma("sync", srd[:, :], tacc[:, :], ssem, deps=[dCt])
            tacc_r.users = [dSt]
            t0 = base * 128
            gsem = "gx0" if own else "gx1"
            dma("gpsimd", lhsT2[0:1, t0:t0 + TOWN].rearrange("a (t c) -> a t c", c=128),
                srd.rearrange("(t s) c -> s t c", s=2)[0:1, :, :], gsem, deps=[dSt])
            d_l2 = dma("gpsimd",
                       lhsT2[1:2, t0:t0 + TOWN].rearrange("a (t c) -> a t c", c=128),
                       srd.rearrange("(t s) c -> s t c", s=2)[1:2, :, :], gsem)
            d_rows = None
            if own:
                dma("sync", rowsMV[0:1, 0:TOWN].rearrange("a (t c) -> a t c", c=128),
                    srd.rearrange("(t s) c -> s t c", s=2)[0:1, :, :], "lx0",
                    deps=[dSt])
                d_rows = dma("sync",
                             rowsMV[0:1, TOWN:2 * TOWN].rearrange("a (t c) -> a t c", c=128),
                             srd.rearrange("(t s) c -> s t c", s=2)[1:2, :, :], "lx0")
            flush_tags[half_id] = (d_l2, dRD, dR1, d_rows)

        dKR = {}
        dVR = {}

        def emit_wsum(dt, bkk, bkv):
            p = dt % 2
            c0 = dt * DTILE
            bank_take(bkk, deps=[dW[dt][0], d_ones])
            for kc in range(KC):
                dm_ = te(lambda E, kc=kc, p=p, bkk=bkk: E.matmul(
                    out=pb[bkk][0:1, 0:DTILE], lhsT=onescol[:, :],
                    rhs=WkB[p][:, kc * DTILE:(kc + 1) * DTILE],
                    start=(kc == 0), stop=(kc == KC - 1)), inc=(kc == KC - 1))
            bank_prod(bkk, dm_)
            wkr[p].users.append(dm_)
            dE = ve(lambda E, bkk=bkk, c0=c0: E.tensor_copy(
                out=krhs2[0:1, c0:c0 + DTILE], in_=pb[bkk][0:1, 0:DTILE]),
                deps=bank_dep(bkk))
            bank_use(bkk, dE)
            dKR[dt] = dE
            bank_take(bkv, deps=[dW[dt][1], d_ones])
            for kc in range(KC):
                dm2 = te(lambda E, kc=kc, p=p, bkv=bkv: E.matmul(
                    out=pb[bkv][0:1, 0:DTILE], lhsT=onescol[:, :],
                    rhs=WvB[p][:, kc * DTILE:(kc + 1) * DTILE],
                    start=(kc == 0), stop=(kc == KC - 1)), inc=(kc == KC - 1))
            bank_prod(bkv, dm2)
            wvr[p].users.append(dm2)
            dE2 = ve(lambda E, bkv=bkv, c0=c0: E.tensor_copy(
                out=vrhs2[0:1, c0:c0 + DTILE], in_=pb[bkv][0:1, 0:DTILE]),
                deps=bank_dep(bkv))
            bank_use(bkv, dE2)
            dVR[dt] = dE2

        def xn_build_tt(tt, banks):
            d_rows = flush_tags[0][3]
            bcast_rows([(rowsMV[0:1, tt * NT:(tt + 1) * NT], Mb, d_rows, mb_r),
                        (rowsMV[0:1, TOWN + tt * NT:TOWN + (tt + 1) * NT], R1b,
                         d_rows, r1_r)], banks=banks)
            dRc = ve(lambda E: E.reciprocal(out=R1b[:, :], in_=R1b[:, :]),
                     deps=[r1_r.prod])
            r1_r.prod = dRc
            cast_deps = []
            for s in range(tt * (NT // 128), (tt + 1) * (NT // 128)):
                cast_deps += d_sl[s]
            for kc in range(KC):
                eng2 = gp
                d1 = eng2(lambda E, kc=kc, tt=tt: E.tensor_tensor(
                    out=xn_own[:, kc * TOWN + tt * NT:kc * TOWN + (tt + 1) * NT],
                    in0=xbf[:, kc * SEQ + tt * NT:kc * SEQ + (tt + 1) * NT],
                    in1=Mb[:, :], op=ALU.add),
                    deps=[mb_r.prod] + cast_deps)
                mb_r.users.append(d1)
                d2 = eng2(lambda E, kc=kc, tt=tt: E.tensor_tensor(
                    out=xn_own[:, kc * TOWN + tt * NT:kc * TOWN + (tt + 1) * NT],
                    in0=xn_own[:, kc * TOWN + tt * NT:kc * TOWN + (tt + 1) * NT],
                    in1=R1b[:, :], op=ALU.mult), deps=[r1_r.prod, d1])
                r1_r.users.append(d2)

        for ps in range(TC // 2):
            emit_slab2(ps, (1, 2, 0, 3))
        flush_half(0)
        load_kv_slices(1, extra=[("ac", P.cur("ac"))])
        emit_wsum(0, 2, 3)


        for dt in range(NDT):
            if dt == 1:
                es_sl.close()
                Wcol.extend(es_q2.enter_context(
                    nc.sbuf_tensor(f"Wcol{i}", [128, KC * 128], BF16, side="right"))
                    for i in range(3))
            if dt == min(2, NDT - 1):
                load_q(0)
                load_q(1)
                load_q(2)
            p = dt % 2
            c0 = dt * DTILE
            kvb = [4 + i for i in range(HPD)]

            pend = None

            def emit_kv(st):
                tc, kbuf, vbuf, dEK, dEV = st
                waits("tensor", [dEK, dEV])
                if tc == 0:
                    for i in kvb:
                        bank_take(i)
                for i in range(HPD):
                    dKV = te(lambda E, i=i, kbuf=kbuf, vbuf=vbuf, tc=tc, kvb=kvb: E.matmul(
                        out=pb[kvb[i]][:, 0:129],
                        lhsT=Ktile[kbuf][:, i * 128:(i + 1) * 128],
                        rhs=Vext[vbuf][:, i * 129:(i + 1) * 129],
                        start=(tc == 0), stop=(tc == 2 * TC - 1)),
                        inc=(i == HPD - 1))
                kt_r[kbuf].users.append(dKV)
                vx_r[vbuf].users.append(dKV)
                if tc == 2 * TC - 1:
                    for i in kvb:
                        bank_prod(i, dKV)

            for tc in range(2 * TC):
                own_side = tc < TC
                fl = flush_tags[0 if own_side else 1]
                kb = tc % 2
                vb = 2 + tc % 2
                kbuf = tc % 2
                vbuf = tc % 2
                bank_take(kb, deps=[dW[dt][0], d_bk, fl[0], dKR[dt]] + d_sl[tc])
                for kc in range(KC):
                    te(lambda E, kc=kc, kb=kb, tc=tc, p=p: E.matmul(
                        out=pb[kb][:, 0:DTILE],
                        lhsT=xbf[:, kc * SEQ + tc * 128:kc * SEQ + tc * 128 + 128],
                        rhs=WkB[p][:, kc * DTILE:(kc + 1) * DTILE],
                        start=(kc == 0), stop=False), inc=False)
                dPK = te(lambda E, kb=kb, c0=c0, tc=tc: E.matmul(
                    out=pb[kb][:, 0:DTILE], lhsT=lhsT2[:, tc * 128:(tc + 1) * 128],
                    rhs=krhs2[:, c0:c0 + DTILE], start=False, stop=True))
                bank_prod(kb, dPK)
                wkr[p].users.append(dPK)
                dEK = sc(lambda E, kb=kb, kbuf=kbuf, tc=tc: E.activation(
                    out=Ktile[kbuf][:, :], in_=pb[kb][:, 0:DTILE], func=AF.Relu,
                    scale=grd[:, tc:tc + 1]),
                    deps=kt_r[kbuf].users + bank_dep(kb) + [fl[1]])
                kt_r[kbuf].users = []
                bank_use(kb, dEK)
                bank_take(vb, deps=[dW[dt][1], d_bv, dVR[dt]] + d_sl[tc])
                for kc in range(KC):
                    te(lambda E, kc=kc, vb=vb, tc=tc, p=p: E.matmul(
                        out=pb[vb][:, 0:DTILE],
                        lhsT=xbf[:, kc * SEQ + tc * 128:kc * SEQ + tc * 128 + 128],
                        rhs=WvB[p][:, kc * DTILE:(kc + 1) * DTILE],
                        start=(kc == 0), stop=False), inc=False)
                dPV = te(lambda E, vb=vb, c0=c0, tc=tc: E.matmul(
                    out=pb[vb][:, 0:DTILE], lhsT=lhsT2[:, tc * 128:(tc + 1) * 128],
                    rhs=vrhs2[:, c0:c0 + DTILE], start=False, stop=True))
                bank_prod(vb, dPV)
                wvr[p].users.append(dPV)
                dEV = ve(lambda E, vb=vb, vbuf=vbuf, tc=tc: E.tensor_scalar(
                    out=Vext[vbuf][:, :].rearrange("p (h c) -> p h c", c=129)[:, :, 0:128],
                    in0=pb[vb][:, 0:DTILE].rearrange("p (h c) -> p h c", c=128),
                    scalar1=gr1[:, tc:tc + 1], scalar2=None, op0=ALU.mult),
                    deps=vx_r[vbuf].users + bank_dep(vb) + [fl[2]])
                vx_r[vbuf].users = []
                bank_use(vb, dEV)
                if pend is not None:
                    emit_kv(pend)
                pend = (tc, kbuf, vbuf, dEK, dEV)

                if dt == 0 and own_side:
                    if tc % 2 == 0:
                        emit_slab2((TC + tc) // 2, (kb, vb, 1 - kb, 5 - vb))
                    if tc == TC - 1:
                        flush_half(1)

            emit_kv(pend)
            if dt + 2 < NDT:
                load_kv_slices(dt + 2)
            if dt + 1 < NDT:
                emit_wsum(dt + 1, 0, 1)
            if dt < NTT:
                xn_build_tt(dt, (0, 1))
            for i in range(HPD):
                h = dt * HPD + i
                dE = ve(lambda E, i=i, h=h, kvb=kvb: E.tensor_copy(
                    out=KVbf[:, h * 129:(h + 1) * 129], in_=pb[kvb[i]][:, 0:129]),
                    deps=bank_dep(kvb[i]))
                bank_use(kvb[i], dE)

        barrier()
        if debug:
            dma("gpsimd", ap["d_vex"][:, :], Vext[(2 * TC - 1) % 2][:, :], "gx0")
            dma("gpsimd", ap["d_kt"][:, :], Ktile[(2 * TC - 1) % 2][:, :], "gx0")
            dma("sync", ap["d_sacc"][:, :], sacc[:, :], "st0")
            dma("sync", ap["d_gdm"][:, :], gdm[:, :], "st0")
            dma("sync", ap["d_gsv"][:, :], gsv[:, :], "st0")
            dma("gpsimd", ap["d_l2"][:, :], lhsT2[:, :], "gx0")
            barrier()
        es_xb.close()
        es3.close()
        es_pre2.close()

        es_g2 = ExitStack()
        xn2T = es_g2.enter_context(nc.sbuf_tensor("xn2T", [128, KC * TOWN], BF16))
        rows_o = es_g2.enter_context(nc.sbuf_tensor("rows_o", [1, TOWN], F32))

        ve(lambda E: E.tensor_tensor(
            out=rows_o[0:1, :], in0=rowsMV[0:1, TOWN:2 * TOWN],
            in1=rowsMV[0:1, TOWN:2 * TOWN], op=ALU.mult))
        ve(lambda E: E.reciprocal(out=rows_o[0:1, :], in_=rows_o[0:1, :]))
        ve(lambda E: E.tensor_scalar(
            out=rows_o[0:1, :], in0=rows_o[0:1, :],
            scalar1=-EPS_LN, scalar2=1.0 + EPS_LN, op0=ALU.mult, op1=ALU.add))
        ve(lambda E: E.reciprocal(out=rows_o[0:1, :], in_=rows_o[0:1, :]))
        es_out = ExitStack()
        outT = es_out.enter_context(nc.sbuf_tensor("outT", [128, KC * TOWN], BF16))

        es7a = ExitStack()
        WcolO = [es7a.enter_context(nc.sbuf_tensor(f"WcolO{i}", [128, KC * 128], BF16))
                 for i in range(3)]
        wo_r = [Res(), Res(), Res()]
        osem = ("ow0", "ow1", "ow2")
        dWo = {}

        def load_wo(do):
            b = do % 3
            dWo[do] = wslice_dma("woT", WcolO[b], do * 128, 128, osem[b], deps=wo_r[b].users)
            wo_r[b].users = []

        es6 = ExitStack()
        e6 = es6.enter_context
        WsB = [e6(nc.sbuf_tensor(f"WsB{i}", [128, KC * 128], F32)) for i in range(2)]
        wsr = [Res(), Res()]
        SgB = [e6(nc.sbuf_tensor(f"SgB{i}", [128, KC * 128], BF16)) for i in range(2)]
        sgr = [Res(), Res()]
        wsem = ("lw0", "lw1")
        dWs = {}

        def load_ws(do):
            pw = do % 2
            dWs[do] = wslice_dma("wsT", WsB[pw], do * 128, 128, wsem[pw],
                                 deps=wsr[pw].users, eng="sync")
            wsr[pw].users = []

        load_ws(0)

        esq = ExitStack()
        eq = esq.enter_context
        QT = eq(nc.sbuf_tensor("QT", [128, KC * TOWN], BF16, side="right"))
        for dq in range(KC):
            b = dq % 3
            for tt in range(NTT):
                bi = (dq * NTT + tt) % 4
                bank_take(bi, deps=[dWq[dq]])
                for kc in range(KC):
                    dP = te(lambda E, kc=kc, bi=bi, b=b, tt=tt: E.matmul(
                        out=pb[bi][:, 0:NT],
                        lhsT=Wcol[b][:, kc * 128:(kc + 1) * 128],
                        rhs=xn_own[:, kc * TOWN + tt * NT:kc * TOWN + (tt + 1) * NT],
                        start=(kc == 0), stop=(kc == KC - 1)), inc=(kc == KC - 1))
                bank_prod(bi, dP)
                wc_r[b].users.append(dP)
                dE = sc(lambda E, bi=bi, dq=dq, tt=tt: E.activation(
                    out=QT[:, dq * TOWN + tt * NT:dq * TOWN + (tt + 1) * NT],
                    in_=pb[bi][:, 0:NT], func=AF.Relu,
                    bias=grids["bq_g"][:, dq:dq + 1]), deps=bank_dep(bi))
                bank_use(bi, dE)
            if dq + 3 < KC:
                load_q(dq + 3)
        barrier()
        esq_q_done = True
        load_wo(0)
        load_wo(1)
        load_wo(2)

        es4s = ExitStack()
        e4s = es4s.enter_context
        otile = [e4s(nc.sbuf_tensor(f"otile{i}", [128, 128], BF16)) for i in range(2)]
        ot_r = [Res(), Res()]
        R12b = e4s(nc.sbuf_tensor("R12b", [128, TOWN], F32))
        r12_r = Res()

        dRS = sc(lambda E: E.sqrt(out=rows_o[0:1, :], in_=rows_o[0:1, :]),
                 deps=[("dv", P.cur("dv"))])
        bcast_rows([(rows_o[0:1, :], R12b, dRS, r12_r)], banks=(6, 7))

        def emit_xn2_chunk(kc):
            eng2 = gp if kc % 2 == 0 else ve
            d1 = eng2(lambda E, kc=kc: E.tensor_tensor(
                out=xn2T[:, kc * TOWN:(kc + 1) * TOWN],
                in0=xn_own[:, kc * TOWN:(kc + 1) * TOWN], in1=R12b[:, :], op=ALU.mult),
                deps=([r12_r.prod] if r12_r.prod else []))
            eng2(lambda E, kc=kc: E.tensor_scalar(
                out=xn2T[:, kc * TOWN:(kc + 1) * TOWN],
                in0=xn2T[:, kc * TOWN:(kc + 1) * TOWN],
                scalar1=grids["lnw_g"][:, kc:kc + 1], scalar2=grids["lnb_g"][:, kc:kc + 1],
                op0=ALU.mult, op1=ALU.add))

        HH = H // 2
        nbk = (HH + 2) // 3

        def emit_num(v):
            tc, hf = v // 2, v % 2
            tloc = tc * 128
            base = nbk * hf
            for j in range(HH):
                h = hf * HH + j
                bi = base + j // 3
                col = (j % 3) * 129
                if j % 3 == 0:
                    bank_take(bi)
                dM = te(lambda E, h=h, bi=bi, col=col, tloc=tloc: E.matmul(
                    out=pb[bi][:, col:col + 129],
                    lhsT=QT[:, h * TOWN + tloc:h * TOWN + tloc + 128],
                    rhs=KVbf[:, h * 129:(h + 1) * 129],
                    start=True, stop=True))
                bank_prod(bi, dM)

        def emit_norm(v):
            tc, hf = v // 2, v % 2
            tloc = tc * 128
            base = nbk * hf
            h0g = hf * HH
            for bi in range(base, base + nbk):
                j0 = (bi - base) * 3
                nh = min(3, HH - j0)
                dS = ve(lambda E, bi=bi, j0=j0, nh=nh, tc=tc, h0g=h0g: E.tensor_scalar(
                    out=sden[:, h0g + j0:h0g + j0 + nh],
                    in0=pb[bi][:, 0:nh * 129].rearrange("p (h c) -> p h c", c=129)[:, :, 128:129],
                    scalar1=decf[:, tc:tc + 1], scalar2=EPS_ATT,
                    op0=ALU.mult, op1=ALU.add), deps=bank_dep(bi))
                bank_use(bi, dS)
            ve(lambda E, h0g=h0g: E.reciprocal(out=stok[:, h0g:h0g + HH],
                                               in_=sden[:, h0g:h0g + HH]))
            dZ = ve(lambda E, tc=tc, h0g=h0g: E.tensor_scalar(
                out=stok[:, h0g:h0g + HH], in0=stok[:, h0g:h0g + HH],
                scalar1=decf[:, tc:tc + 1], scalar2=None, op0=ALU.mult))
            for j in range(HH):
                h = h0g + j
                bi = base + j // 3
                col = (j % 3) * 129
                ob = j % 2
                dO = ve(lambda E, bi=bi, col=col, h=h, ob=ob: E.tensor_scalar(
                    out=otile[ob][:, :], in0=pb[bi][:, col:col + 128],
                    scalar1=stok[:, h:h + 1], scalar2=None, op0=ALU.mult),
                    deps=ot_r[ob].users + [dZ])
                ot_r[ob].users = []
                bank_use(bi, dO)
                tb = 6 + ob
                bank_take(tb, deps=[dO, d_idb])
                dT = te(lambda E, ob=ob, tb=tb: E.matmul(
                    out=pb[tb][:, 0:128], lhsT=otile[ob][:, :], rhs=identb[:, :],
                    start=True, stop=True))
                bank_prod(tb, dT)
                ot_r[ob].users.append(dT)
                dC = sc(lambda E, h=h, tb=tb, tloc=tloc: E.copy(
                    out=outT[:, h * TOWN + tloc:h * TOWN + tloc + 128],
                    in_=pb[tb][:, 0:128]), deps=bank_dep(tb))
                bank_use(tb, dC)
            emit_xn2_chunk(v)

        prev = None
        for v in range(2 * TC):
            emit_num(v)
            if prev is not None:
                emit_norm(prev)
            prev = v
        emit_norm(prev)
        d_xn2 = ("gp", P.cur("gp"))
        d_xn2v = ("dv", P.cur("dv"))
        barrier()
        es4s.close()
        esq.close()
        es_g = ExitStack()
        gRaw = es_g.enter_context(nc.sbuf_tensor("gRaw", [128, KC * TOWN], BF16,
                                                 side="right"))

        for do in range(KC):
            if do + 1 < KC:
                load_ws(do + 1)
            pw = do % 2
            dSg = sc(lambda E, pw=pw: E.sign(out=SgB[pw][:, :], in_=WsB[pw][:, :]),
                     deps=sgr[pw].users + [dWs[do]])
            sgr[pw].users = []
            dMx = ve(lambda E, do=do, pw=pw: E.reduce_max(
                out=maxg[:, do:do + 1], in_=WsB[pw][:, :], axis=AX.X,
                apply_absolute_value=True), deps=[dWs[do]])
            wsr[pw].users += [dSg, dMx]
            for tt in range(NTT):
                bi = (do * NTT + tt) % 4
                bank_take(bi, deps=[dSg, d_xn2, d_xn2v])
                for kc in range(KC):
                    dP = te(lambda E, kc=kc, bi=bi, tt=tt, pw=pw: E.matmul(
                        out=pb[bi][:, 0:NT],
                        lhsT=SgB[pw][:, kc * 128:(kc + 1) * 128],
                        rhs=xn2T[:, kc * TOWN + tt * NT:kc * TOWN + (tt + 1) * NT],
                        start=(kc == 0), stop=(kc == KC - 1)), inc=(kc == KC - 1))
                bank_prod(bi, dP)
                sgr[pw].users.append(dP)
                dE = ve(lambda E, bi=bi, do=do, tt=tt: E.tensor_copy(
                    out=gRaw[:, do * TOWN + tt * NT:do * TOWN + (tt + 1) * NT],
                    in_=pb[bi][:, 0:NT]), deps=bank_dep(bi))
                bank_use(bi, dE)
        ve(lambda E: E.reduce_max(out=maxg[:, KC:KC + 1], in_=maxg[:, 0:KC], axis=AX.X))
        bank_take(6, deps=[("dv", P.cur("dv"))])
        dT = te(lambda E: E.transpose(out=pb[6][0:1, 0:128], in_=maxg[:, KC:KC + 1],
                                      identity=idents[:, :]))
        bank_prod(6, dT)
        dC = ve(lambda E: E.tensor_copy(out=s11r[0:1, 0:128], in_=pb[6][0:1, 0:128]),
                deps=bank_dep(6))
        bank_use(6, dC)
        dM2 = ve(lambda E: E.reduce_max(out=s11r[0:1, 128:129], in_=s11r[0:1, 0:128],
                                        axis=AX.X))
        bank_take(7, deps=[dM2])
        dB = te(lambda E: E.matmul(out=pb[7][:, 0:1], lhsT=ones1f[:, :],
                                   rhs=s11r[0:1, 128:129], start=True, stop=True))
        bank_prod(7, dB)
        dC2 = ve(lambda E: E.tensor_copy(out=s11c[:, :], in_=pb[7][:, 0:1]),
                 deps=bank_dep(7))
        bank_use(7, dC2)
        ve(lambda E: E.tensor_scalar(
            out=gsc_g[:, :], in0=grids["gam_g"][:, :], scalar1=s11c[:, :], scalar2=None,
            op0=ALU.mult))
        barrier()
        es6.close()

        es7b = ExitStack()
        e7 = es7b.enter_context
        outF = [e7(nc.sbuf_tensor(f"outF{i}", [128, TOWN], F32)) for i in range(2)]
        of_r = [Res(), Res()]
        xres = [e7(nc.sbuf_tensor(f"xres{i}", [128, TOWN], F32)) for i in range(2)]
        xr_r = [Res(), Res()]
        dsil = {}
        for do in range(KC):
            for tt in range(NTT):
                dsil[(do, tt)] = sc(lambda E, do=do, tt=tt: E.activation(
                    out=gRaw[:, do * TOWN + tt * NT:do * TOWN + (tt + 1) * NT],
                    in_=gRaw[:, do * TOWN + tt * NT:do * TOWN + (tt + 1) * NT],
                    func=AF.Silu, scale=gsc_g[:, do:do + 1],
                    bias=grids["bet_g"][:, do:do + 1]))
        for do in range(KC):
            buf = do % 2
            b3 = do % 3
            dX = dma("scalar", xres[buf][:, :], ap["xT"][do * 128:(do + 1) * 128, :],
                     "lr0" if buf == 0 else "lr1", deps=xr_r[buf].users)
            xr_r[buf].users = []
            for tt in range(NTT):
                bi = (do * NTT + tt) % 4
                bank_take(bi, deps=[dWo[do]])
                for kc in range(KC):
                    dP = te(lambda E, kc=kc, bi=bi, b3=b3, tt=tt: E.matmul(
                        out=pb[bi][:, 0:NT],
                        lhsT=WcolO[b3][:, kc * 128:(kc + 1) * 128],
                        rhs=outT[:, kc * TOWN + tt * NT:kc * TOWN + (tt + 1) * NT],
                        start=(kc == 0), stop=(kc == KC - 1)), inc=(kc == KC - 1))
                bank_prod(bi, dP)
                wo_r[b3].users.append(dP)
                d1 = ve(lambda E, buf=buf, tt=tt, do=do, bi=bi: E.scalar_tensor_tensor(
                    out=outF[buf][:, tt * NT:(tt + 1) * NT], in0=pb[bi][:, 0:NT],
                    scalar=grids["bo_g"][:, do:do + 1],
                    in1=gRaw[:, do * TOWN + tt * NT:do * TOWN + (tt + 1) * NT],
                    op0=ALU.add, op1=ALU.mult),
                    deps=of_r[buf].users + [dsil[(do, tt)]] + bank_dep(bi))
                of_r[buf].users = []
                bank_use(bi, d1)
                d2 = ve(lambda E, buf=buf, tt=tt: E.tensor_tensor(
                    out=outF[buf][:, tt * NT:(tt + 1) * NT],
                    in0=outF[buf][:, tt * NT:(tt + 1) * NT],
                    in1=xres[buf][:, tt * NT:(tt + 1) * NT], op=ALU.add), deps=[dX])
                xr_r[buf].users.append(d2)
                of_r[buf].users.append(d2)
            dSt = dma("sync", ap["yT"][do * 128:(do + 1) * 128, :], outF[buf][:, :],
                      "st0" if buf == 0 else "st1", deps=of_r[buf].users)
            of_r[buf].users.append(dSt)
            if do + 3 < KC:
                load_wo(do + 3)

        if debug:
            barrier()
            dbg_g0 = P.cur("gx0")
            dma("gpsimd", ap["d_xn"][:, :], xn_own[:, :], "gx0")
            dma("gpsimd", ap["d_kv"][:, :], KVbf[:, :], "gx0")
            dma("gpsimd", ap["d_xn2"][:, :], xn2T[:, :], "gx0")
            dma("gpsimd", ap["d_out"][:, :], outT[:, :], "gx0")
            dma("gpsimd", ap["d_graw"][:, :], gRaw[:, :], "gx0")
        W("sync", "st0", P.cur("st0"))
        W("sync", "st1", P.cur("st1"))
        if debug:
            W("sync", "gx0", P.cur("gx0"))
        es7b.close()
        es_g.close()
        es_q2.close()
        es7a.close()
        es_out.close()
        es_g2.close()
        es_pre.close()

        with nc.Block() as block:
            @block.sync
            def _(E):
                for fn in P.ops["sync"]:
                    fn(E)

            @block.gpsimd
            def _(E):
                for fn in P.ops["gpsimd"]:
                    fn(E)

            @block.vector
            def _(E):
                for fn in P.ops["vector"]:
                    fn(E)

            @block.scalar
            def _(E):
                for fn in P.ops["scalar"]:
                    fn(E)

            @block.tensor
            def _(E):
                for fn in P.ops["tensor"]:
                    fn(E)

    return nc



def stage_inputs(x, Wq, bq, Wk, bk, Wv, bv, Wo, bo, Ws, gam, bet, lnw, lnb,
                 D, TOWN, n_cores):
    KC = D // 128
    TC = TOWN // 128
    c = np.ascontiguousarray
    shared = {
        "wqT": c(Wq.T), "wkT": c(Wk.T), "wvT": c(Wv.T), "woT": c(Wo.T), "wsT": c(Ws.T),
        "bq_g": c(bq.reshape(KC, 128).T), "bo_g": c(bo.reshape(KC, 128).T),
        "lnw_g": c(lnw.reshape(KC, 128).T), "lnb_g": c(lnb.reshape(KC, 128).T),
        "gam_g": c(gam.reshape(KC, 128).T), "bet_g": c(bet.reshape(KC, 128).T),
        "bk_r": c(bk.reshape(1, D)), "bv_r": c(bv.reshape(1, D)),
        "ident": np.eye(128, dtype=np.float32),
    }
    in_maps = []
    for core in range(n_cores):
        b, half = core // 2, core % 2
        t0, t1 = half * TOWN, (1 - half) * TOWN
        pos = np.zeros((128, 2 * TC), dtype=np.float32)
        for i in range(TC):
            pos[:, i] = t0 + i * 128 + np.arange(128)
            pos[:, TC + i] = t1 + i * 128 + np.arange(128)
        m = dict(shared)
        m["xT"] = c(x[b, t0:t0 + TOWN, :].T)
        m["xoT"] = c(x[b, t1:t1 + TOWN, :].T)
        m["posf_g"] = pos
        in_maps.append(m)
    return in_maps


def kernel(L0_attn_x, L0_attn_Wq, L0_attn_bq, L0_attn_Wk, L0_attn_bk,
           L0_attn_Wv, L0_attn_bv, L0_attn_Wo, L0_attn_bo,
           L0_ffn_weight, L0_ffn_gamma, L0_ffn_beta, L0_ffn_ln_weight, L0_ffn_ln_bias):
    D, TOWN = 2048, 1024
    f = lambda a: np.asarray(a, dtype=np.float32)
    x = f(L0_attn_x)
    in_maps = stage_inputs(
        x, f(L0_attn_Wq), f(L0_attn_bq), f(L0_attn_Wk), f(L0_attn_bk),
        f(L0_attn_Wv), f(L0_attn_bv), f(L0_attn_Wo), f(L0_attn_bo),
        f(L0_ffn_weight), f(L0_ffn_gamma), f(L0_ffn_beta),
        f(L0_ffn_ln_weight), f(L0_ffn_ln_bias), D=D, TOWN=TOWN, n_cores=N_CORES)
    nc = build_program(D=D, TOWN=TOWN)
    res = run_bass_kernel_spmd(nc, in_maps, core_ids=list(range(N_CORES)))
    B, S = L0_attn_x.shape[0], L0_attn_x.shape[1]
    out = np.empty((B, S, D), dtype=np.float32)
    for core in range(N_CORES):
        b, half = core // 2, core % 2
        t0 = half * TOWN
        out[b, t0:t0 + TOWN, :] = np.asarray(res.results[core]["yT"]).T
    return out
